# version 01
# baseline (speedup 1.0000x reference)
import sys
import numpy as np

sys.path.insert(0, '/opt/trn_rl_repo')
import concourse.bass as bass
import concourse.mybir as mybir
from concourse.bass_utils import run_bass_kernel_spmd
import ml_dtypes
from contextlib import ExitStack

F32 = mybir.dt.float32
BF16 = mybir.dt.bfloat16
I32 = mybir.dt.int32
AF = mybir.ActivationFunctionType
ALU = mybir.AluOpType
AX = mybir.AxisListType
BF = ml_dtypes.bfloat16

NC = 8
S = 8192
TC = 1024
NT = 8
DM = 4096
DQ = 1024
F0 = 3072
PROJ = 10304
CONV = 6144
H, PP, NN, G, KK = 64, 64, 128, 8, 4


class Prog:
    def __init__(self):
        self.ops = []

    def dma(self, fn):
        self.ops.append(("sync", fn, 16))

    def mm(self, fn):
        self.ops.append(("tensor", fn, 1))

    def vec(self, fn):
        self.ops.append(("vector", fn, 1))

    def act(self, fn):
        self.ops.append(("scalar", fn, 1))

    def emit(self, nc, sem, blk):
        pre = []
        c = 0
        for (e, fn, inc) in self.ops:
            pre.append(c)
            c += inc
        total = c

        def mk(name):
            def inner(eng):
                for i, (e, fn, inc) in enumerate(self.ops):
                    if e == name:
                        if pre[i] > 0:
                            eng.wait_ge(sem, pre[i])
                        fn(eng).then_inc(sem, inc)
            return inner

        blk.sync(mk("sync"))
        blk.tensor(mk("tensor"))
        blk.vector(mk("vector"))
        blk.scalar(mk("scalar"))

        def gps(eng):
            mk("gpsimd")(eng)
            eng.wait_ge(sem, total)
        blk.gpsimd(gps)


class Alloc:
    def __init__(self, nc, es):
        self.nc, self.es = nc, es

    def sb(self, name, shape, dt):
        return self.es.enter_context(self.nc.sbuf_tensor(name, shape, dt))

    def ps(self, name, shape, dt):
        return self.es.enter_context(self.nc.psum_tensor(name, shape, dt))

    def sem(self, name):
        return self.es.enter_context(self.nc.semaphore(name))

    def block(self):
        return self.es.enter_context(self.nc.Block())


def _r(t, pat, **kw):
    return t.rearrange(pat, **kw)


def build_p0():
    nc = bass.Bass("TRN2", target_bir_lowering=False)
    wqkv_s = nc.dram_tensor("wqkv_s", [512, F0], F32, kind="ExternalInput").ap()
    wo_s = nc.dram_tensor("wo_s", [128, DM], F32, kind="ExternalInput").ap()
    wbr_s = nc.dram_tensor("wbr_s", [128, 1024], F32, kind="ExternalInput").ap()
    ip_s = nc.dram_tensor("ip_s", [1408, DM], F32, kind="ExternalInput").ap()
    op_s = nc.dram_tensor("op_s", [512, DM], F32, kind="ExternalInput").ap()
    qw_s = nc.dram_tensor("qw_s", [512, DM], I32, kind="ExternalInput").ap()
    sc_s = nc.dram_tensor("sc_s", [512, 32], F32, kind="ExternalInput").ap()
    zp_s = nc.dram_tensor("zp_s", [512, 32], F32, kind="ExternalInput").ap()
    identb = nc.dram_tensor("identb", [128, 128], BF16, kind="ExternalInput").ap()

    wqkv_o = nc.dram_tensor("wqkv_o", [512, F0], BF16, kind="ExternalOutput").ap()
    wo_o = nc.dram_tensor("wo_o", [128, DM], BF16, kind="ExternalOutput").ap()
    wbr_o = nc.dram_tensor("wbr_o", [128, 1024], BF16, kind="ExternalOutput").ap()
    ipT_o = nc.dram_tensor("ipT_o", [DM, 1408], BF16, kind="ExternalOutput").ap()
    opT_o = nc.dram_tensor("opT_o", [DM, 512], BF16, kind="ExternalOutput").ap()
    f1T_o = nc.dram_tensor("f1T_o", [DM, 512], BF16, kind="ExternalOutput").ap()

    p = Prog()
    with ExitStack() as es:
        al = Alloc(nc, es)
        tb = al.sb("tb", [128, DM], F32)
        tbb = al.sb("tbb", [128, DM], BF16)
        qb = al.sb("qb", [128, DM], I32)
        qf = al.sb("qf", [128, DM], F32)
        scb = al.sb("scb", [128, 32], F32)
        zpb = al.sb("zpb", [128, 32], F32)
        idb = al.sb("idb", [128, 128], BF16)
        ot = al.sb("ot", [128, 128], BF16)
        pstb = al.ps("pstb", [128, 128], BF16)
        sem = al.sem("sem")
        blk = al.block()
        p.dma(lambda e: e.dma_start(idb[:], identb[:]))
        for b in range(4):
            p.dma(lambda e, b=b: e.dma_start(tb[:, 0:F0], wqkv_s[b * 128:(b + 1) * 128, :]))
            p.vec(lambda e: e.tensor_copy(tbb[:, 0:F0], tb[:, 0:F0]))
            p.dma(lambda e, b=b: e.dma_start(wqkv_o[b * 128:(b + 1) * 128, :], tbb[:, 0:F0]))
        p.dma(lambda e: e.dma_start(tb[:], wo_s[:]))
        p.vec(lambda e: e.tensor_copy(tbb[:], tb[:]))
        p.dma(lambda e: e.dma_start(wo_o[:], tbb[:]))
        p.dma(lambda e: e.dma_start(tb[:, 0:1024], wbr_s[:]))
        p.vec(lambda e: e.tensor_copy(tbb[:, 0:1024], tb[:, 0:1024]))
        p.dma(lambda e: e.dma_start(wbr_o[:], tbb[:, 0:1024]))

        def transpose_out(src_ap, out_hbm, b):
            for cc in range(32):
                p.mm(lambda e, cc=cc: e.transpose(pstb[:], src_ap[:, cc * 128:(cc + 1) * 128], idb[:]))
                p.vec(lambda e: e.tensor_copy(ot[:], pstb[:]))
                p.dma(lambda e, cc=cc, b=b: e.dma_start(
                    out_hbm[cc * 128:(cc + 1) * 128, b * 128:(b + 1) * 128], ot[:]))

        for b in range(11):
            p.dma(lambda e, b=b: e.dma_start(tb[:], ip_s[b * 128:(b + 1) * 128, :]))
            p.vec(lambda e: e.tensor_copy(tbb[:], tb[:]))
            transpose_out(tbb, ipT_o, b)
        for b in range(4):
            p.dma(lambda e, b=b: e.dma_start(tb[:], op_s[b * 128:(b + 1) * 128, :]))
            p.vec(lambda e: e.tensor_copy(tbb[:], tb[:]))
            transpose_out(tbb, opT_o, b)
        for b in range(4):
            p.dma(lambda e, b=b: e.dma_start(qb[:], qw_s[b * 128:(b + 1) * 128, :]))
            p.dma(lambda e, b=b: e.dma_start(scb[:], sc_s[b * 128:(b + 1) * 128, :]))
            p.dma(lambda e, b=b: e.dma_start(zpb[:], zp_s[b * 128:(b + 1) * 128, :]))
            p.vec(lambda e: e.tensor_copy(qf[:], qb[:]))
            for g in range(32):
                p.vec(lambda e, g=g: e.tensor_scalar(
                    tbb[:, g * 128:(g + 1) * 128], qf[:, g * 128:(g + 1) * 128],
                    scb[:, g:g + 1], zpb[:, g:g + 1], op0=ALU.mult, op1=ALU.add))
            transpose_out(tbb, f1T_o, b)
        p.emit(nc, sem, blk)
    return nc


def build_a():
    nc = bass.Bass("TRN2", target_bir_lowering=False)
    x_s = nc.dram_tensor("x_s", [TC, DM], F32, kind="ExternalInput").ap()
    wqkv_b = nc.dram_tensor("wqkv_b", [DM, F0], BF16, kind="ExternalInput").ap()
    bqkv = nc.dram_tensor("bqkv", [1, F0], F32, kind="ExternalInput").ap()
    wcat = nc.dram_tensor("wcat", [24, DM], F32, kind="ExternalInput").ap()
    bases = nc.dram_tensor("bases", [1, 24], F32, kind="ExternalInput").ap()
    gatein = nc.dram_tensor("gatein", [1, 1], F32, kind="ExternalInput").ap()
    eye16 = nc.dram_tensor("eye16", [1, 16], F32, kind="ExternalInput").ap()
    wbr_b = nc.dram_tensor("wbr_b", [1024, 1024], BF16, kind="ExternalInput").ap()
    identb = nc.dram_tensor("identb", [128, 128], BF16, kind="ExternalInput").ap()
    identf = nc.dram_tensor("identf", [128, 128], F32, kind="ExternalInput").ap()

    qkt_o = nc.dram_tensor("qkt_o", [2048, TC], BF16, kind="ExternalOutput").ap()
    vn_o = nc.dram_tensor("vn_o", [TC, DQ], BF16, kind="ExternalOutput").ap()
    part_o = nc.dram_tensor("part_o", [TC, DM], F32, kind="ExternalOutput").ap()

    p = Prog()
    with ExitStack() as es:
        al = Alloc(nc, es)
        xs = al.sb("xs", [128, DM], F32)
        sq = al.sb("sq", [128, DM], F32)
        xnT_ = al.sb("xnT", [128, 32 * TC], BF16)
        mcol = al.sb("mcol", [128, 8], F32)
        rstd = al.sb("rstd", [128, 8], F32)
        rmsr = al.sb("rmsr", [128, 8], F32)
        tcol = al.sb("tcol", [128, 1], F32)
        t2col = al.sb("t2col", [128, 1], F32)
        wbuf_ = al.sb("wbuf", [128, 32 * 128], BF16)
        qksb = al.sb("qksb", [128, 512], BF16)
        vtile = al.sb("vtile", [128, 128], BF16)
        onesf = al.sb("onesf", [1, 128], F32)
        onesb = al.sb("onesb", [1, 512], BF16)
        bq_b = al.sb("bq_b", [1, F0], BF16)
        rowf = al.sb("rowf", [1, F0], F32)
        wcat_f = al.sb("wcat_f", [24, DM], F32)
        wcat_b = al.sb("wcat_b", [24, DM], BF16)
        wcatT = al.sb("wcatT", [128, 32 * 24], BF16)
        logitsT = al.sb("logitsT", [24, TC], F32)
        lgn = al.sb("lgn", [128, 24], F32)
        base_bc = al.sb("base_bc", [128, 24], F32)
        eye_bc = al.sb("eye_bc", [128, 16], F32)
        Zb = al.sb("Zb", [128, 128], F32)
        A4 = al.sb("A4", [128, 128], F32)
        E4 = al.sb("E4", [128, 128], F32)
        u4 = al.sb("u4", [128, 32], F32)
        v4 = al.sb("v4", [128, 32], F32)
        m4 = al.sb("m4", [128, 32], F32)
        s4 = al.sb("s4", [128, 32], F32)
        l4 = al.sb("l4", [128, 32], F32)
        hpre = al.sb("hpre", [128, 32], F32)
        hpost = al.sb("hpost", [128, 32], F32)
        hscr = al.sb("hscr", [128, 32], F32)
        gcol = al.sb("gcol", [128, 1], F32)
        gm1 = al.sb("gm1", [128, 1], F32)
        xp = al.sb("xp", [128, 1024], F32)
        xpb = al.sb("xpb", [128, 1024], BF16)
        xpT = al.sb("xpT", [128, 1024], BF16)
        ysb = al.sb("ysb", [128, 1024], F32)
        wbr_sb_ = al.sb("wbr_sb", [128, 8 * 1024], BF16)
        pt = al.sb("pt", [128, DM], F32)
        idb = al.sb("idb", [128, 128], BF16)
        idf = al.sb("idf", [128, 128], F32)
        psQ = al.ps("psQ", [128, 512], F32)
        psTf = al.ps("psTf", [128, 128], F32)
        psTb = al.ps("psTb", [128, 128], BF16)
        sem = al.sem("sem")
        blk = al.block()
        xnT = _r(xnT_[:, :], "p (cc t) -> p cc t", cc=32)
        wbuf = _r(wbuf_[:, :], "p (cc f) -> p cc f", cc=32)
        wbr_sb = _r(wbr_sb_[:, :], "p (cc f) -> p cc f", cc=8)

        p.dma(lambda e: e.dma_start(idb[:], identb[:]))
        p.dma(lambda e: e.dma_start(idf[:], identf[:]))
        p.vec(lambda e: e.memset(onesf[:], 1.0))
        p.vec(lambda e: e.memset(onesb[:], 1.0))

        for i in range(NT):
            p.dma(lambda e, i=i: e.dma_start(xs[:], x_s[i * 128:(i + 1) * 128, :]))
            p.vec(lambda e: e.reduce_sum(tcol[:], xs[:], axis=AX.X))
            p.vec(lambda e, i=i: e.tensor_scalar(mcol[:, i:i + 1], tcol[:], 1.0 / DM, None, op0=ALU.mult))
            p.vec(lambda e, i=i: e.tensor_scalar(xs[:], xs[:], mcol[:, i:i + 1], None, op0=ALU.subtract))
            p.act(lambda e: e.activation(sq[:], xs[:], AF.Square, accum_out=tcol[:]))
            p.vec(lambda e: e.tensor_scalar(t2col[:], tcol[:], 1.0 / DM, 1e-5, op0=ALU.mult, op1=ALU.add))
            p.act(lambda e: e.activation(tcol[:], t2col[:], AF.Ln))
            p.act(lambda e, i=i: e.activation(rstd[:, i:i + 1], tcol[:], AF.Exp, scale=-0.5))
            p.vec(lambda e, i=i: e.tensor_scalar(xs[:], xs[:], rstd[:, i:i + 1], None, op0=ALU.mult))
            p.act(lambda e: e.activation(sq[:], xs[:], AF.Square, accum_out=tcol[:]))
            p.vec(lambda e: e.tensor_scalar(t2col[:], tcol[:], 1.0 / DM, 1.1920929e-07, op0=ALU.mult, op1=ALU.add))
            p.act(lambda e: e.activation(tcol[:], t2col[:], AF.Ln))
            p.act(lambda e, i=i: e.activation(rmsr[:, i:i + 1], tcol[:], AF.Exp, scale=-0.5))
            for cc in range(32):
                p.mm(lambda e, cc=cc: e.transpose(psTf[:], xs[:, cc * 128:(cc + 1) * 128], idf[:]))
                p.vec(lambda e, cc=cc, i=i: e.tensor_copy(xnT[:, cc, i * 128:(i + 1) * 128], psTf[:]))

        p.dma(lambda e: e.dma_start(wcat_f[:], wcat[:]))
        p.vec(lambda e: e.tensor_copy(wcat_b[:], wcat_f[:]))
        for cc in range(32):
            p.mm(lambda e, cc=cc: e.transpose(psTb[0:128, 0:24], wcat_b[:, cc * 128:(cc + 1) * 128], idb[0:24, 0:24]))
            p.vec(lambda e, cc=cc: e.tensor_copy(wcatT[:, cc * 24:(cc + 1) * 24], psTb[0:128, 0:24]))
        p.dma(lambda e: e.dma_start(rowf[0:1, 0:24], bases[:]))
        p.mm(lambda e: e.matmul(psTf[:, 0:24], onesf[:], rowf[0:1, 0:24], start=True, stop=True))
        p.vec(lambda e: e.tensor_copy(base_bc[:], psTf[:, 0:24]))
        p.dma(lambda e: e.dma_start(rowf[0:1, 0:16], eye16[:]))
        p.mm(lambda e: e.matmul(psTf[:, 0:16], onesf[:], rowf[0:1, 0:16], start=True, stop=True))
        p.vec(lambda e: e.tensor_copy(eye_bc[:], psTf[:, 0:16]))
        for tg in range(2):
            for cc in range(32):
                p.mm(lambda e, cc=cc, tg=tg: e.matmul(
                    psQ[0:24, :], wcatT[:, cc * 24:(cc + 1) * 24],
                    xnT[:, cc, tg * 512:(tg + 1) * 512], start=(cc == 0), stop=(cc == 31)))
            p.vec(lambda e, tg=tg: e.tensor_copy(logitsT[:, tg * 512:(tg + 1) * 512], psQ[0:24, :]))
        for i in range(NT):
            p.mm(lambda e, i=i: e.transpose(psTf[:, 0:24], logitsT[:, i * 128:(i + 1) * 128], idf[0:24, 0:24]))
            p.vec(lambda e, i=i: e.scalar_tensor_tensor(
                lgn[:], psTf[:, 0:24], rmsr[:, i:i + 1], base_bc[:], op0=ALU.mult, op1=ALU.add))
            p.vec(lambda e, i=i: e.tensor_scalar(Zb[:, i * 16:(i + 1) * 16], lgn[:, 0:16], 10.0, None, op0=ALU.mult))
            p.vec(lambda e, i=i: e.tensor_copy(hpre[:, i * 4:(i + 1) * 4], lgn[:, 16:20]))
            p.vec(lambda e, i=i: e.tensor_copy(hpost[:, i * 4:(i + 1) * 4], lgn[:, 20:24]))

        Zv = _r(Zb[:, :], "p (s i j) -> p s i j", i=4, j=4)
        A4v = _r(A4[:, :], "p (s i j) -> p s i j", i=4, j=4)
        A4ji = _r(A4[:, :], "p (s i j) -> p s j i", i=4, j=4)
        E4v = _r(E4[:, :], "p (s i j) -> p s i j", i=4, j=4)
        E4ji = _r(E4[:, :], "p (s i j) -> p s j i", i=4, j=4)
        m4i = _r(m4[:, :], "p (s i) -> p s i", i=4)
        s4i = _r(s4[:, :], "p (s i) -> p s i", i=4)
        u4i = _r(u4[:, :], "p (s i) -> p s i", i=4)
        v4j = _r(v4[:, :], "p (s j) -> p s j", j=4)
        vb = v4j.unsqueeze(2).broadcast_to((128, 8, 4, 4))
        ub = u4i.unsqueeze(3).broadcast_to((128, 8, 4, 4))
        mb = m4i.unsqueeze(3).broadcast_to((128, 8, 4, 4))

        p.vec(lambda e: e.memset(v4[:], 0.0))
        for it in range(50):
            p.vec(lambda e: e.tensor_tensor(A4v, Zv, vb, op=ALU.add))
            p.vec(lambda e: e.reduce_max(m4i, A4v, axis=AX.X))
            p.vec(lambda e: e.tensor_tensor(A4v, A4v, mb, op=ALU.subtract))
            p.act(lambda e: e.activation(E4[:], A4[:], AF.Exp))
            p.vec(lambda e: e.tensor_reduce(s4i, E4v, axis=AX.X, op=ALU.add))
            p.act(lambda e: e.activation(l4[:], s4[:], AF.Ln))
            p.vec(lambda e: e.tensor_tensor(l4[:], l4[:], m4[:], op=ALU.add))
            p.vec(lambda e: e.tensor_scalar(u4[:], l4[:], -1.0, None, op0=ALU.mult))
            p.vec(lambda e: e.tensor_tensor(A4v, Zv, ub, op=ALU.add))
            p.vec(lambda e: e.reduce_max(m4i, A4ji, axis=AX.X))
            p.vec(lambda e: e.tensor_tensor(A4ji, A4ji, mb, op=ALU.subtract))
            p.act(lambda e: e.activation(E4[:], A4[:], AF.Exp))
            p.vec(lambda e: e.tensor_reduce(s4i, E4ji, axis=AX.X, op=ALU.add))
            p.act(lambda e: e.activation(l4[:], s4[:], AF.Ln))
            p.vec(lambda e: e.tensor_tensor(l4[:], l4[:], m4[:], op=ALU.add))
            p.vec(lambda e: e.tensor_scalar(v4[:], l4[:], -1.0, None, op0=ALU.mult))
        p.vec(lambda e: e.tensor_tensor(A4v, Zv, ub, op=ALU.add))
        p.vec(lambda e: e.tensor_tensor(A4v, A4v, vb, op=ALU.add))
        p.act(lambda e: e.activation(Zb[:], A4[:], AF.Exp))
        p.dma(lambda e: e.dma_start(rowf[0:1, 0:1], gatein[:]))
        p.mm(lambda e: e.matmul(psTf[:, 0:1], onesf[:], rowf[0:1, 0:1], start=True, stop=True))
        p.act(lambda e: e.activation(gcol[:], psTf[:, 0:1], AF.Sigmoid))
        p.act(lambda e: e.activation(gm1[:], gcol[:], AF.Identity, scale=-1.0, bias=1.0))
        p.vec(lambda e: e.tensor_scalar(Zb[:], Zb[:], gcol[:], None, op0=ALU.mult))
        Zs16 = _r(Zb[:, :], "p (s k) -> p s k", k=16)
        eyeb = eye_bc[:, :].unsqueeze(1).broadcast_to((128, 8, 16))
        p.vec(lambda e: e.scalar_tensor_tensor(Zs16, eyeb, gm1[:], Zs16, op0=ALU.mult, op1=ALU.add))
        for hl in (hpre, hpost):
            hv = _r(hl[:, :], "p (s j) -> p s j", j=4)
            hm = _r(hscr[:, :], "p (s j) -> p s j", j=4)
            mb8 = m4[:, 0:8].unsqueeze(2).broadcast_to((128, 8, 4))
            lb8 = l4[:, 0:8].unsqueeze(2).broadcast_to((128, 8, 4))
            p.vec(lambda e, hv=hv: e.reduce_max(m4[:, 0:8], hv, axis=AX.X))
            p.vec(lambda e, hv=hv, mb8=mb8: e.tensor_tensor(hv, hv, mb8, op=ALU.subtract))
            p.act(lambda e, hl=hl: e.activation(hscr[:], hl[:], AF.Exp))
            p.vec(lambda e, hm=hm: e.tensor_reduce(s4[:, 0:8], hm, axis=AX.X, op=ALU.add))
            p.vec(lambda e: e.reciprocal(l4[:, 0:8], s4[:, 0:8]))
            p.vec(lambda e, hv=hv, hm=hm, lb8=lb8: e.tensor_tensor(hv, hm, lb8, op=ALU.mult))

        p.dma(lambda e: e.dma_start(rowf[:], bqkv[:]))
        p.vec(lambda e: e.tensor_copy(bq_b[:], rowf[:]))
        wq_view = _r(wqkv_b, "(cc p) f -> p cc f", p=128)
        for f in range(24):
            p.dma(lambda e, f=f: e.dma_start(wbuf[:, 0:16, :], wq_view[:, 0:16, f * 128:(f + 1) * 128]))
            p.dma(lambda e, f=f: e.dma_start(wbuf[:, 16:32, :], wq_view[:, 16:32, f * 128:(f + 1) * 128]))
            for tg in range(2):
                p.mm(lambda e, f=f: e.matmul(psQ[:], bq_b[0:1, f * 128:(f + 1) * 128], onesb[:], start=True, stop=False))
                for cc in range(32):
                    p.mm(lambda e, cc=cc, tg=tg: e.matmul(
                        psQ[:], wbuf[:, cc, :], xnT[:, cc, tg * 512:(tg + 1) * 512],
                        start=False, stop=(cc == 31)))
                if f < 16:
                    p.vec(lambda e: e.tensor_copy(qksb[:], psQ[:]))
                    p.dma(lambda e, f=f, tg=tg: e.dma_start(
                        qkt_o[f * 128:(f + 1) * 128, tg * 512:(tg + 1) * 512], qksb[:]))
                else:
                    p.vec(lambda e: e.tensor_copy(qksb[:], psQ[:]))
                    for j in range(4):
                        p.mm(lambda e, j=j: e.transpose(psTb[:], qksb[:, j * 128:(j + 1) * 128], idb[:]))
                        p.vec(lambda e: e.tensor_copy(vtile[:], psTb[:]))
                        p.dma(lambda e, f=f, tg=tg, j=j: e.dma_start(
                            vn_o[tg * 512 + j * 128: tg * 512 + (j + 1) * 128,
                                 (f - 16) * 128:(f - 15) * 128], vtile[:]))

        wbrv = _r(wbr_b, "(cc p) f -> p cc f", p=128)
        p.dma(lambda e: e.dma_start(wbr_sb[:, :, :], wbrv[:, :, :]))
        for i in range(NT):
            p.dma(lambda e, i=i: e.dma_start(xs[:], x_s[i * 128:(i + 1) * 128, :]))
            p.vec(lambda e, i=i: e.tensor_scalar(sq[:], xs[:], mcol[:, i:i + 1], None, op0=ALU.subtract))
            p.vec(lambda e, i=i: e.tensor_scalar(sq[:], sq[:], rstd[:, i:i + 1], None, op0=ALU.mult))
            p.vec(lambda e, i=i: e.tensor_scalar(xp[:], sq[:, 0:1024], hpre[:, i * 4:i * 4 + 1], None, op0=ALU.mult))
            for n in range(1, 4):
                p.vec(lambda e, i=i, n=n: e.scalar_tensor_tensor(
                    xp[:], sq[:, n * 1024:(n + 1) * 1024], hpre[:, i * 4 + n:i * 4 + n + 1],
                    xp[:], op0=ALU.mult, op1=ALU.add))
            p.vec(lambda e: e.tensor_copy(xpb[:], xp[:]))
            for cc in range(8):
                p.mm(lambda e, cc=cc: e.transpose(psTb[:], xpb[:, cc * 128:(cc + 1) * 128], idb[:]))
                p.vec(lambda e, cc=cc: e.tensor_copy(xpT[:, cc * 128:(cc + 1) * 128], psTb[:]))
            for co in range(2):
                for cc in range(8):
                    p.mm(lambda e, cc=cc, co=co: e.matmul(
                        psQ[:], xpT[:, cc * 128:(cc + 1) * 128], wbr_sb[:, cc, co * 512:(co + 1) * 512],
                        start=(cc == 0), stop=(cc == 7)))
                p.vec(lambda e, co=co: e.tensor_copy(ysb[:, co * 512:(co + 1) * 512], psQ[:]))
            for n in range(4):
                p.vec(lambda e, i=i, n=n: e.scalar_tensor_tensor(
                    pt[:, n * 1024:(n + 1) * 1024], sq[:, 0:1024],
                    Zb[:, i * 16 + n * 4:i * 16 + n * 4 + 1], xs[:, n * 1024:(n + 1) * 1024],
                    op0=ALU.mult, op1=ALU.add))
                for j in range(1, 4):
                    p.vec(lambda e, i=i, n=n, j=j: e.scalar_tensor_tensor(
                        pt[:, n * 1024:(n + 1) * 1024], sq[:, j * 1024:(j + 1) * 1024],
                        Zb[:, i * 16 + n * 4 + j:i * 16 + n * 4 + j + 1],
                        pt[:, n * 1024:(n + 1) * 1024], op0=ALU.mult, op1=ALU.add))
                p.vec(lambda e, i=i, n=n: e.scalar_tensor_tensor(
                    pt[:, n * 1024:(n + 1) * 1024], ysb[:],
                    hpost[:, i * 4 + n:i * 4 + n + 1], pt[:, n * 1024:(n + 1) * 1024],
                    op0=ALU.mult, op1=ALU.add))
            p.dma(lambda e, i=i: e.dma_start(part_o[i * 128:(i + 1) * 128, :], pt[:]))
        p.emit(nc, sem, blk)
    return nc


def build_b():
    nc = bass.Bass("TRN2", target_bir_lowering=False)
    qt_s = nc.dram_tensor("qt_s", [DQ, TC], BF16, kind="ExternalInput").ap()
    kt_f = nc.dram_tensor("kt_f", [DQ, S], BF16, kind="ExternalInput").ap()
    vn_f = nc.dram_tensor("vn_f", [S, DQ], BF16, kind="ExternalInput").ap()
    part_s = nc.dram_tensor("part_s", [TC, DM], F32, kind="ExternalInput").ap()
    wo_b = nc.dram_tensor("wo_b", [DQ, DM], BF16, kind="ExternalInput").ap()
    bo = nc.dram_tensor("bo", [1, DM], F32, kind="ExternalInput").ap()
    identb = nc.dram_tensor("identb", [128, 128], BF16, kind="ExternalInput").ap()

    x1_o = nc.dram_tensor("x1_o", [TC, DM], F32, kind="ExternalOutput").ap()
    xn1t_o = nc.dram_tensor("xn1t_o", [DM, TC], BF16, kind="ExternalOutput").ap()

    p = Prog()
    with ExitStack() as es:
        al = Alloc(nc, es)
        qt_ = al.sb("qt", [128, 8 * TC], BF16)
        ktb_ = al.sb("ktb", [128, 8 * 512], BF16)
        vb = al.sb("vb", [128, DQ], BF16)
        Sf = al.sb("Sf", [128, S], F32)
        Pf = al.sb("Pf", [128, S], F32)
        ones1b = al.sb("ones1b", [1, 128], BF16)
        negm = al.sb("negm", [128, 1], F32)
        sumc = al.sb("sumc", [128, 1], F32)
        rden = al.sb("rden", [128, 1], F32)
        pts = al.sb("pts", [128, 128], BF16)
        osb = al.sb("osb", [128, DQ], BF16)
        otT_ = al.sb("otT", [128, 8 * 128], BF16)
        wob_ = al.sb("wob", [128, 8 * 512], BF16)
        bo_b = al.sb("bo_b", [1, DM], BF16)
        rowf = al.sb("rowf", [1, DM], F32)
        onesb = al.sb("onesb", [1, 512], BF16)
        partt = al.sb("partt", [128, DM], F32)
        x1sb = al.sb("x1sb", [128, DM], F32)
        sq = al.sb("sq", [128, DM], F32)
        xnb = al.sb("xnb", [128, DM], BF16)
        tcol = al.sb("tcol", [128, 1], F32)
        t2col = al.sb("t2col", [128, 1], F32)
        xtile = al.sb("xtile", [128, 128], BF16)
        idb = al.sb("idb", [128, 128], BF16)
        psS = al.ps("psS", [128, 512], F32)
        psO0 = al.ps("psO0", [128, 512], F32)
        psO1 = al.ps("psO1", [128, 512], F32)
        psTb = al.ps("psTb", [128, 128], BF16)
        sem = al.sem("sem")
        blk = al.block()
        qt = _r(qt_[:, :], "p (dc t) -> p dc t", dc=8)
        ktb = _r(ktb_[:, :], "p (dc k) -> p dc k", dc=8)
        otT = _r(otT_[:, :], "p (dc q) -> p dc q", dc=8)
        wob = _r(wob_[:, :], "p (dc c) -> p dc c", dc=8)
        kt_v = _r(kt_f, "(dc p) k -> p dc k", p=128)
        wo_v = _r(wo_b, "(dc p) c -> p dc c", p=128)

        p.dma(lambda e: e.dma_start(idb[:], identb[:]))
        p.vec(lambda e: e.memset(onesb[:], 1.0))
        p.vec(lambda e: e.memset(ones1b[:], 1.0))
        p.dma(lambda e: e.dma_start(rowf[:], bo[:]))
        p.vec(lambda e: e.tensor_copy(bo_b[:], rowf[:]))
        p.dma(lambda e: e.dma_start(qt[:, :, :], _r(qt_s, "(dc p) t -> p dc t", p=128)[:, :, :]))

        for qi in range(NT):
            for g in range(16):
                p.dma(lambda e, g=g: e.dma_start(ktb[:, :, :], kt_v[:, :, g * 512:(g + 1) * 512]))
                for dc in range(8):
                    p.mm(lambda e, dc=dc, qi=qi: e.matmul(
                        psS[:], qt[:, dc, qi * 128:(qi + 1) * 128], ktb[:, dc, :],
                        start=(dc == 0), stop=(dc == 7)))
                p.vec(lambda e, g=g: e.tensor_scalar(
                    Sf[:, g * 512:(g + 1) * 512], psS[:], 0.03125, None, op0=ALU.mult))
            p.vec(lambda e: e.reduce_max(negm[:], Sf[:], axis=AX.X, negate=True))
            p.act(lambda e: e.activation(Pf[:], Sf[:], AF.Exp, bias=negm[:], accum_out=sumc[:]))
            p.vec(lambda e: e.reciprocal(rden[:], sumc[:]))
            for kv in range(64):
                p.dma(lambda e, kv=kv: e.dma_start(vb[:], vn_f[kv * 128:(kv + 1) * 128, :]))
                p.vec(lambda e, kv=kv: e.tensor_copy(pts[:], Pf[:, kv * 128:(kv + 1) * 128]))
                p.mm(lambda e: e.transpose(psTb[:], pts[:], idb[:]))
                p.vec(lambda e: e.tensor_copy(pts[:], psTb[:]))
                p.mm(lambda e, kv=kv: e.matmul(psO0[:], pts[:], vb[:, 0:512],
                                               start=(kv == 0), stop=(kv == 63)))
                p.mm(lambda e, kv=kv: e.matmul(psO1[:], pts[:], vb[:, 512:1024],
                                               start=(kv == 0), stop=(kv == 63)))
            p.vec(lambda e: e.tensor_scalar(osb[:, 0:512], psO0[:], rden[:], None, op0=ALU.mult))
            p.vec(lambda e: e.tensor_scalar(osb[:, 512:1024], psO1[:], rden[:], None, op0=ALU.mult))
            for dc in range(8):
                p.mm(lambda e, dc=dc: e.transpose(psTb[:], osb[:, dc * 128:(dc + 1) * 128], idb[:]))
                p.vec(lambda e, dc=dc: e.tensor_copy(otT[:, dc, :], psTb[:]))
            p.dma(lambda e, qi=qi: e.dma_start(partt[:], part_s[qi * 128:(qi + 1) * 128, :]))
            for cg in range(8):
                p.dma(lambda e, cg=cg: e.dma_start(wob[:, :, :], wo_v[:, :, cg * 512:(cg + 1) * 512]))
                p.mm(lambda e, cg=cg: e.matmul(psS[:], ones1b[:], bo_b[0:1, cg * 512:(cg + 1) * 512],
                                               start=True, stop=False))
                for dc in range(8):
                    p.mm(lambda e, dc=dc: e.matmul(psS[:], otT[:, dc, :], wob[:, dc, :],
                                                   start=False, stop=(dc == 7)))
                p.vec(lambda e, cg=cg: e.tensor_tensor(
                    x1sb[:, cg * 512:(cg + 1) * 512], psS[:], partt[:, cg * 512:(cg + 1) * 512], op=ALU.add))
            p.dma(lambda e, qi=qi: e.dma_start(x1_o[qi * 128:(qi + 1) * 128, :], x1sb[:]))
            p.vec(lambda e: e.reduce_sum(tcol[:], x1sb[:], axis=AX.X))
            p.vec(lambda e: e.tensor_scalar(tcol[:], tcol[:], 1.0 / DM, None, op0=ALU.mult))
            p.vec(lambda e: e.tensor_scalar(x1sb[:], x1sb[:], tcol[:], None, op0=ALU.subtract))
            p.act(lambda e: e.activation(sq[:], x1sb[:], AF.Square, accum_out=tcol[:]))
            p.vec(lambda e: e.tensor_scalar(t2col[:], tcol[:], 1.0 / DM, 1e-5, op0=ALU.mult, op1=ALU.add))
            p.act(lambda e: e.activation(tcol[:], t2col[:], AF.Ln))
            p.act(lambda e: e.activation(t2col[:], tcol[:], AF.Exp, scale=-0.5))
            p.vec(lambda e: e.tensor_scalar(x1sb[:], x1sb[:], t2col[:], None, op0=ALU.mult))
            p.vec(lambda e: e.tensor_copy(xnb[:], x1sb[:]))
            for cc in range(32):
                p.mm(lambda e, cc=cc: e.transpose(psTb[:], xnb[:, cc * 128:(cc + 1) * 128], idb[:]))
                p.vec(lambda e: e.tensor_copy(xtile[:], psTb[:]))
                p.dma(lambda e, cc=cc, qi=qi: e.dma_start(
                    xn1t_o[cc * 128:(cc + 1) * 128, qi * 128:(qi + 1) * 128], xtile[:]))
        p.emit(nc, sem, blk)
    return nc


def build_c1():
    nc = bass.Bass("TRN2", target_bir_lowering=False)
    xn1t_s = nc.dram_tensor("xn1t_s", [DM, TC], BF16, kind="ExternalInput").ap()
    ipT = nc.dram_tensor("ipT", [DM, PROJ], BF16, kind="ExternalInput").ap()

    gatet_o = nc.dram_tensor("gatet_o", [DM, TC], BF16, kind="ExternalOutput").ap()
    hbct_o = nc.dram_tensor("hbct_o", [CONV, TC], BF16, kind="ExternalOutput").ap()
    dtt_o = nc.dram_tensor("dtt_o", [64, TC], F32, kind="ExternalOutput").ap()

    p = Prog()
    with ExitStack() as es:
        al = Alloc(nc, es)
        xnT_ = al.sb("xnT", [128, 32 * TC], BF16)
        ipw_ = al.sb("ipw", [128, 32 * 128], BF16)
        evb = al.sb("evb", [128, 512], BF16)
        evf = al.sb("evf", [64, 512], F32)
        psP = al.ps("psP", [128, 512], F32)
        sem = al.sem("sem")
        blk = al.block()
        xnT = _r(xnT_[:, :], "p (cc t) -> p cc t", cc=32)
        ipw = _r(ipw_[:, :], "p (cc m) -> p cc m", cc=32)
        xv = _r(xn1t_s, "(cc p) t -> p cc t", p=128)
        ipv = _r(ipT, "(cc p) m -> p cc m", p=128)
        for q in range(4):
            p.dma(lambda e, q=q: e.dma_start(xnT[:, q * 8:(q + 1) * 8, :], xv[:, q * 8:(q + 1) * 8, :]))
        for pb in range(81):
            m = 128 if pb < 80 else 64
            p.dma(lambda e, pb=pb, m=m: e.dma_start(ipw[:, 0:16, 0:m], ipv[:, 0:16, pb * 128:pb * 128 + m]))
            p.dma(lambda e, pb=pb, m=m: e.dma_start(ipw[:, 16:32, 0:m], ipv[:, 16:32, pb * 128:pb * 128 + m]))
            for tg in range(2):
                for cc in range(32):
                    p.mm(lambda e, cc=cc, tg=tg, m=m: e.matmul(
                        psP[0:m, :], ipw[:, cc, 0:m], xnT[:, cc, tg * 512:(tg + 1) * 512],
                        start=(cc == 0), stop=(cc == 31)))
                if pb < 32:
                    p.vec(lambda e, m=m: e.tensor_copy(evb[0:m, :], psP[0:m, :]))
                    p.dma(lambda e, pb=pb, tg=tg: e.dma_start(
                        gatet_o[pb * 128:(pb + 1) * 128, tg * 512:(tg + 1) * 512], evb[:]))
                elif pb < 80:
                    p.vec(lambda e, m=m: e.tensor_copy(evb[0:m, :], psP[0:m, :]))
                    p.dma(lambda e, pb=pb, tg=tg: e.dma_start(
                        hbct_o[(pb - 32) * 128:(pb - 31) * 128, tg * 512:(tg + 1) * 512], evb[:]))
                else:
                    p.vec(lambda e: e.tensor_copy(evf[:], psP[0:64, :]))
                    p.dma(lambda e, tg=tg: e.dma_start(
                        dtt_o[:, tg * 512:(tg + 1) * 512], evf[:]))
        p.emit(nc, sem, blk)
    return nc


def build_c2a():
    nc = bass.Bass("TRN2", target_bir_lowering=False)
    halo = nc.dram_tensor("halo", [CONV, TC + 3], BF16, kind="ExternalInput").ap()
    cw_in = nc.dram_tensor("cw_in", [128, 48 * 4], F32, kind="ExternalInput").ap()
    cb_in = nc.dram_tensor("cb_in", [128, 48], F32, kind="ExternalInput").ap()
    dtt_s = nc.dram_tensor("dtt_s", [64, TC], F32, kind="ExternalInput").ap()
    dtb_in = nc.dram_tensor("dtb_in", [64, 1], F32, kind="ExternalInput").ap()
    alog_in = nc.dram_tensor("alog_in", [64, 1], F32, kind="ExternalInput").ap()
    identb = nc.dram_tensor("identb", [128, 128], BF16, kind="ExternalInput").ap()
    identf = nc.dram_tensor("identf", [128, 128], F32, kind="ExternalInput").ap()

    xbct_o = nc.dram_tensor("xbct_o", [CONV, TC], BF16, kind="ExternalOutput").ap()
    snat_o = nc.dram_tensor("snat_o", [TC, 64], F32, kind="ExternalOutput").ap()
    dtpn_o = nc.dram_tensor("dtpn_o", [TC, 64], F32, kind="ExternalOutput").ap()
    sloc_o = nc.dram_tensor("sloc_o", [8 * 128, DM], BF16, kind="ExternalOutput").ap()
    mprime_o = nc.dram_tensor("mprime_o", [128, DM], BF16, kind="ExternalOutput").ap()
    cumdec_o = nc.dram_tensor("cumdec_o", [64, 9], F32, kind="ExternalOutput").ap()

    p = Prog()
    with ExitStack() as es:
        al = Alloc(nc, es)
        xbc_ = al.sb("xbc", [128, 48 * TC], BF16)
        hal = al.sb("hal", [128, TC + 3], BF16)
        cwb = al.sb("cwb", [128, 48 * 4], F32)
        cbb = al.sb("cbb", [128, 48], F32)
        cbuf = al.sb("cbuf", [128, TC], F32)
        dtT = al.sb("dtT", [64, TC], F32)
        dtb = al.sb("dtb", [64, 1], F32)
        acol = al.sb("acol", [64, 1], F32)
        zb = al.sb("zb", [64, 128], F32)
        rl = al.sb("rl", [64, 128], F32)
        ab = al.sb("ab", [64, 128], F32)
        eb = al.sb("eb", [64, 128], F32)
        lb = al.sb("lb", [64, 128], F32)
        dtpT = al.sb("dtpT", [64, 128], F32)
        am = al.sb("am", [64, 128], F32)
        sT = al.sb("sT", [64, 128], F32)
        z64 = al.sb("z64", [64, 128], F32)
        wtm = al.sb("wtm", [64, 128], F32)
        wT = al.sb("wT", [64, 128], F32)
        snat = al.sb("snat", [128, 64], F32)
        dtpn = al.sb("dtpn", [128, 64], F32)
        wnat = al.sb("wnat", [128, 64], F32)
        dcol = al.sb("dcol", [64, 1], F32)
        cum = al.sb("cum", [64, 9], F32)
        drow = al.sb("drow", [1, 64], F32)
        Dbc = al.sb("Dbc", [128, 64], F32)
        Sst = al.sb("Sst", [128, DM], F32)
        Sb = al.sb("Sb", [128, DM], BF16)
        bnat = al.sb("bnat", [128, 128], BF16)
        xnat = al.sb("xnat", [128, 64], BF16)
        dtxb = al.sb("dtxb", [128, 64], BF16)
        dtxw = al.sb("dtxw", [128, 64], BF16)
        onesf = al.sb("onesf", [1, 128], F32)
        idb = al.sb("idb", [128, 128], BF16)
        idf = al.sb("idf", [128, 128], F32)
        psC = al.ps("psC", [128, 64], F32)
        psRow = al.ps("psRow", [1, 128], F32)
        psBc = al.ps("psBc", [128, 64], F32)
        psTf = al.ps("psTf", [128, 64], F32)
        psTb = al.ps("psTb", [128, 128], BF16)
        sem = al.sem("sem")
        blk = al.block()
        xbc = _r(xbc_[:, :], "p (ch t) -> p ch t", ch=48)
        p.dma(lambda e: e.dma_start(idb[:], identb[:]))
        p.dma(lambda e: e.dma_start(idf[:], identf[:]))
        p.vec(lambda e: e.memset(onesf[:], 1.0))
        p.vec(lambda e: e.memset(z64[:], 0.0))
        p.dma(lambda e: e.dma_start(cwb[:], cw_in[:]))
        p.dma(lambda e: e.dma_start(cbb[:], cb_in[:]))
        p.dma(lambda e: e.dma_start(dtT[:], dtt_s[:]))
        p.dma(lambda e: e.dma_start(dtb[:], dtb_in[:]))
        p.dma(lambda e: e.dma_start(acol[:], alog_in[:]))
        p.act(lambda e: e.activation(acol[:], acol[:], AF.Exp))
        p.vec(lambda e: e.tensor_scalar(acol[:], acol[:], -1.0, None, op0=ALU.mult))
        for ch in range(48):
            p.dma(lambda e, ch=ch: e.dma_start(hal[:], halo[ch * 128:(ch + 1) * 128, :]))
            p.vec(lambda e, ch=ch: e.tensor_scalar(
                cbuf[:], hal[:, 0:TC], cwb[:, ch * 4:ch * 4 + 1], cbb[:, ch:ch + 1],
                op0=ALU.mult, op1=ALU.add))
            for k in range(1, 4):
                p.vec(lambda e, ch=ch, k=k: e.scalar_tensor_tensor(
                    cbuf[:], hal[:, k:k + TC], cwb[:, ch * 4 + k:ch * 4 + k + 1], cbuf[:],
                    op0=ALU.mult, op1=ALU.add))
            p.act(lambda e, ch=ch: e.activation(xbc[:, ch, :], cbuf[:], AF.Silu))
            p.dma(lambda e, ch=ch: e.dma_start(xbct_o[ch * 128:(ch + 1) * 128, :], xbc[:, ch, :]))
        p.vec(lambda e: e.memset(Sst[:], 0.0))
        p.vec(lambda e: e.memset(cum[:, 0:1], 1.0))
        for k in range(8):
            sl = slice(k * 128, (k + 1) * 128)
            p.vec(lambda e, sl=sl: e.tensor_scalar(zb[:], dtT[:, sl], dtb[:], None, op0=ALU.add))
            p.vec(lambda e: e.tensor_scalar(rl[:], zb[:], 0.0, None, op0=ALU.max))
            p.act(lambda e: e.activation(ab[:], zb[:], AF.Abs))
            p.act(lambda e: e.activation(eb[:], ab[:], AF.Exp, scale=-1.0))
            p.vec(lambda e: e.tensor_scalar(eb[:], eb[:], 1.0, None, op0=ALU.add))
            p.act(lambda e: e.activation(lb[:], eb[:], AF.Ln))
            p.vec(lambda e: e.tensor_tensor(dtpT[:], rl[:], lb[:], op=ALU.add))
            p.vec(lambda e: e.tensor_scalar(am[:], dtpT[:], acol[:], None, op0=ALU.mult))
            p.vec(lambda e: e.tensor_tensor_scan(sT[:], am[:], z64[:], 0.0, op0=ALU.add, op1=ALU.add))
            p.mm(lambda e: e.transpose(psTf[:], sT[:], idf[0:64, 0:64]))
            p.vec(lambda e: e.tensor_copy(snat[:], psTf[:]))
            p.dma(lambda e, sl=sl: e.dma_start(snat_o[sl, :], snat[:]))
            p.mm(lambda e: e.transpose(psTf[:], dtpT[:], idf[0:64, 0:64]))
            p.vec(lambda e: e.tensor_copy(dtpn[:], psTf[:]))
            p.dma(lambda e, sl=sl: e.dma_start(dtpn_o[sl, :], dtpn[:]))
            p.act(lambda e: e.activation(dcol[:], sT[:, 127:128], AF.Exp))
            p.vec(lambda e, k=k: e.tensor_tensor(cum[:, k + 1:k + 2], cum[:, k:k + 1], dcol[:], op=ALU.mult))
            p.vec(lambda e: e.tensor_scalar(wtm[:], sT[:], sT[:, 127:128], None, op0=ALU.subtract))
            p.act(lambda e: e.activation(wT[:], wtm[:], AF.Exp, scale=-1.0))
            p.mm(lambda e: e.transpose(psTf[:], wT[:], idf[0:64, 0:64]))
            p.vec(lambda e: e.tensor_copy(wnat[:], psTf[:]))
            p.mm(lambda e: e.transpose(psRow[0:1, 0:64], dcol[:], idf[0:64, 0:64]))
            p.vec(lambda e: e.tensor_copy(drow[:], psRow[0:1, 0:64]))
            p.mm(lambda e: e.matmul(psBc[:], onesf[:], drow[:], start=True, stop=True))
            p.vec(lambda e: e.tensor_copy(Dbc[:], psBc[:]))
            p.vec(lambda e: e.tensor_copy(Sb[:], Sst[:]))
            p.dma(lambda e, k=k: e.dma_start(sloc_o[k * 128:(k + 1) * 128, :], Sb[:]))
            for g in range(8):
                p.mm(lambda e, g=g, sl=sl: e.transpose(psTb[:], xbc[:, 32 + g, sl], idb[:]))
                p.vec(lambda e: e.tensor_copy(bnat[:], psTb[:]))
                for hh in range(8):
                    h = g * 8 + hh
                    pb = (h % 2) * 64
                    p.mm(lambda e, h=h, pb=pb, sl=sl: e.transpose(
                        psTb[0:128, 0:64], xbc_[pb:pb + 64, :].rearrange(
                            "p (ch t) -> p ch t", ch=48)[:, h // 2, sl], idb[pb:pb + 64, pb:pb + 64]))
                    p.vec(lambda e: e.tensor_copy(xnat[:], psTb[0:128, 0:64]))
                    p.vec(lambda e, h=h: e.tensor_scalar(dtxb[:], xnat[:], dtpn[:, h:h + 1], None, op0=ALU.mult))
                    p.vec(lambda e, h=h: e.tensor_scalar(dtxw[:], dtxb[:], wnat[:, h:h + 1], None, op0=ALU.mult))
                    p.mm(lambda e: e.matmul(psC[:], bnat[:], dtxw[:], start=True, stop=True))
                    p.vec(lambda e, h=h: e.scalar_tensor_tensor(
                        Sst[:, h * 64:(h + 1) * 64], Sst[:, h * 64:(h + 1) * 64],
                        Dbc[:, h:h + 1], psC[:], op0=ALU.mult, op1=ALU.add))
        p.vec(lambda e: e.tensor_copy(Sb[:], Sst[:]))
        p.dma(lambda e: e.dma_start(mprime_o[:], Sb[:]))
        p.dma(lambda e: e.dma_start(cumdec_o[:], cum[:]))
        p.emit(nc, sem, blk)
    return nc


def build_c2b():
    nc = bass.Bass("TRN2", target_bir_lowering=False)
    xbct_s = nc.dram_tensor("xbct_s", [CONV, TC], BF16, kind="ExternalInput").ap()
    gatet_s = nc.dram_tensor("gatet_s", [DM, TC], BF16, kind="ExternalInput").ap()
    snat_s = nc.dram_tensor("snat_s", [TC, 64], F32, kind="ExternalInput").ap()
    dtpn_s = nc.dram_tensor("dtpn_s", [TC, 64], F32, kind="ExternalInput").ap()
    sloc_s = nc.dram_tensor("sloc_s", [8 * 128, DM], BF16, kind="ExternalInput").ap()
    prevM = nc.dram_tensor("prevM", [7 * 128, DM], BF16, kind="ExternalInput").ap()
    prevD = nc.dram_tensor("prevD", [1, 7 * 64], F32, kind="ExternalInput").ap()
    cumd = nc.dram_tensor("cumd", [1, 9 * 64], F32, kind="ExternalInput").ap()
    d_in = nc.dram_tensor("d_in", [1, 64], F32, kind="ExternalInput").ap()
    nw_in = nc.dram_tensor("nw_in", [1, DM], F32, kind="ExternalInput").ap()
    triu_in = nc.dram_tensor("triu_in", [128, 128], F32, kind="ExternalInput").ap()
    identb = nc.dram_tensor("identb", [128, 128], BF16, kind="ExternalInput").ap()
    identf = nc.dram_tensor("identf", [128, 128], F32, kind="ExternalInput").ap()

    ynt_o = nc.dram_tensor("ynt_o", [DM, TC], BF16, kind="ExternalOutput").ap()

    p = Prog()
    with ExitStack() as es:
        al = Alloc(nc, es)
        xch_ = al.sb("xch", [128, 48 * 128], BF16)
        gch_ = al.sb("gch", [128, 32 * 128], BF16)
        Sinit = al.sb("Sinit", [128, DM], F32)
        Sin = al.sb("Sin", [128, DM], BF16)
        slocb = al.sb("slocb", [128, DM], BF16)
        pmb = al.sb("pmb", [128, DM], BF16)
        rowbuf = al.sb("rowbuf", [1, 576], F32)
        drow = al.sb("drow", [1, 64], F32)
        Dbcj = al.sb("Dbcj", [128, 64], F32)
        DbcD = al.sb("DbcD", [128, 64], F32)
        cumbc = al.sb("cumbc", [128, 64], F32)
        nwbc = al.sb("nwbc", [128, DM], F32)
        nwrow = al.sb("nwrow", [1, DM], F32)
        triu = al.sb("triu", [128, 128], F32)
        snc = al.sb("snc", [128, 64], F32)
        negs = al.sb("negs", [128, 64], F32)
        dtpn = al.sb("dtpn", [128, 64], F32)
        gm_ = al.sb("gm", [128, 8 * 128], F32)
        srow = al.sb("srow", [1, 128], F32)
        dbuf = al.sb("dbuf", [128, 128], F32)
        ebuf = al.sb("ebuf", [128, 128], F32)
        mtb = al.sb("mtb", [128, 128], BF16)
        epb = al.sb("epb", [128, 128], BF16)
        chb = al.sb("chb", [128, 128], BF16)
        xnat = al.sb("xnat", [128, 64], BF16)
        dtxb = al.sb("dtxb", [128, 64], BF16)
        ytb = al.sb("ytb", [64, 128], BF16)
        ytile = al.sb("ytile", [128, DM], F32)
        sqscr = al.sb("sqscr", [128, DM], F32)
        gnat = al.sb("gnat", [128, DM], BF16)
        ybf = al.sb("ybf", [128, DM], BF16)
        ms8 = al.sb("ms8", [128, 8], F32)
        rstd8 = al.sb("rstd8", [128, 8], F32)
        ytt = al.sb("ytt", [128, 128], BF16)
        onesf = al.sb("onesf", [1, 128], F32)
        idb = al.sb("idb", [128, 128], BF16)
        idf = al.sb("idf", [128, 128], F32)
        psG = al.ps("psG", [128, 128], F32)
        psB = al.ps("psB", [128, 128], F32)
        psRow = al.ps("psRow", [1, 128], F32)
        psY = al.ps("psY", [64, 128], F32)
        psTb = al.ps("psTb", [128, 128], BF16)
        psN = al.ps("psN", [128, 512], F32)
        sem = al.sem("sem")
        blk = al.block()
        xch = _r(xch_[:, :], "p (ch t) -> p ch t", ch=48)
        gch = _r(gch_[:, :], "p (cc t) -> p cc t", cc=32)
        gm = gm_
        p.dma(lambda e: e.dma_start(idb[:], identb[:]))
        p.dma(lambda e: e.dma_start(idf[:], identf[:]))
        p.dma(lambda e: e.dma_start(triu[:], triu_in[:]))
        p.vec(lambda e: e.memset(onesf[:], 1.0))
        p.dma(lambda e: e.dma_start(nwrow[:], nw_in[:]))
        for g in range(8):
            p.mm(lambda e, g=g: e.matmul(psN[:], onesf[:], nwrow[0:1, g * 512:(g + 1) * 512], start=True, stop=True))
            p.vec(lambda e, g=g: e.tensor_copy(nwbc[:, g * 512:(g + 1) * 512], psN[:]))
        p.dma(lambda e: e.dma_start(drow[:], d_in[:]))
        p.mm(lambda e: e.matmul(psB[:, 0:64], onesf[:], drow[:], start=True, stop=True))
        p.vec(lambda e: e.tensor_copy(DbcD[:], psB[:, 0:64]))
        p.vec(lambda e: e.memset(Sinit[:], 0.0))
        p.dma(lambda e: e.dma_start(rowbuf[0:1, 0:448], prevD[:]))
        for j in range(7):
            p.dma(lambda e, j=j: e.dma_start(pmb[:], prevM[j * 128:(j + 1) * 128, :]))
            p.mm(lambda e, j=j: e.matmul(psB[:, 0:64], onesf[:], rowbuf[0:1, j * 64:(j + 1) * 64], start=True, stop=True))
            p.vec(lambda e: e.tensor_copy(Dbcj[:], psB[:, 0:64]))
            for h in range(64):
                p.vec(lambda e, h=h: e.scalar_tensor_tensor(
                    Sinit[:, h * 64:(h + 1) * 64], Sinit[:, h * 64:(h + 1) * 64],
                    Dbcj[:, h:h + 1], pmb[:, h * 64:(h + 1) * 64], op0=ALU.mult, op1=ALU.add))
        p.dma(lambda e: e.dma_start(rowbuf[:], cumd[:]))

        for k in range(8):
            sl = slice(k * 128, (k + 1) * 128)
            xbv = _r(xbct_s, "(ch p) t -> p ch t", p=128)
            gtv = _r(gatet_s, "(cc p) t -> p cc t", p=128)
            for q in range(3):
                p.dma(lambda e, sl=sl, q=q, xbv=xbv: e.dma_start(
                    xch[:, q * 16:(q + 1) * 16, :], xbv[:, q * 16:(q + 1) * 16, sl]))
            for q in range(2):
                p.dma(lambda e, sl=sl, q=q, gtv=gtv: e.dma_start(
                    gch[:, q * 16:(q + 1) * 16, :], gtv[:, q * 16:(q + 1) * 16, sl]))
            p.dma(lambda e, k=k: e.dma_start(slocb[:], sloc_s[k * 128:(k + 1) * 128, :]))
            p.dma(lambda e, sl=sl: e.dma_start(snc[:], snat_s[sl, :]))
            p.dma(lambda e, sl=sl: e.dma_start(dtpn[:], dtpn_s[sl, :]))
            p.mm(lambda e, k=k: e.matmul(psB[:, 0:64], onesf[:], rowbuf[0:1, k * 64:(k + 1) * 64], start=True, stop=True))
            p.vec(lambda e: e.tensor_copy(cumbc[:], psB[:, 0:64]))
            for h in range(64):
                p.vec(lambda e, h=h: e.scalar_tensor_tensor(
                    Sin[:, h * 64:(h + 1) * 64], Sinit[:, h * 64:(h + 1) * 64],
                    cumbc[:, h:h + 1], slocb[:, h * 64:(h + 1) * 64], op0=ALU.mult, op1=ALU.add))
            p.vec(lambda e: e.tensor_scalar(negs[:], snc[:], -1.0, None, op0=ALU.mult))
            for g in range(8):
                p.mm(lambda e, g=g: e.matmul(psG[:], xch[:, 32 + g, :], xch[:, 40 + g, :], start=True, stop=True))
                p.vec(lambda e, g=g: e.tensor_tensor(gm[:, g * 128:(g + 1) * 128], psG[:], triu[:], op=ALU.mult))
            for h in range(64):
                g = h // 8
                pb = (h % 2) * 64
                p.mm(lambda e, h=h: e.transpose(psRow[:], snc[:, h:h + 1], idf[:]))
                p.vec(lambda e: e.tensor_copy(srow[:], psRow[:]))
                p.mm(lambda e: e.matmul(psB[:], onesf[:], srow[:], start=True, stop=True))
                p.vec(lambda e, h=h: e.scalar_tensor_tensor(
                    dbuf[:], psB[:], negs[:, h:h + 1], triu[:], op0=ALU.add, op1=ALU.mult))
                p.act(lambda e: e.activation(ebuf[:], dbuf[:], AF.Exp))
                p.vec(lambda e, g=g: e.tensor_tensor(mtb[:], gm[:, g * 128:(g + 1) * 128], ebuf[:], op=ALU.mult))
                p.act(lambda e: e.activation(epb[:], psB[:], AF.Exp))
                p.vec(lambda e, g=g: e.tensor_tensor(chb[:], xch[:, 40 + g, :], epb[:], op=ALU.mult))
                p.mm(lambda e, h=h, pb=pb: e.transpose(
                    psTb[0:128, 0:64], xch_[pb:pb + 64, :].rearrange(
                        "p (ch t) -> p ch t", ch=48)[:, h // 2, :], idb[pb:pb + 64, pb:pb + 64]))
                p.vec(lambda e: e.tensor_copy(xnat[:], psTb[0:128, 0:64]))
                p.vec(lambda e, h=h: e.tensor_scalar(dtxb[:], xnat[:], dtpn[:, h:h + 1], None, op0=ALU.mult))
                p.mm(lambda e: e.matmul(psY[:], dtxb[:], mtb[:], start=True, stop=False))
                p.mm(lambda e, h=h: e.matmul(psY[:], Sin[:, h * 64:(h + 1) * 64], chb[:], start=False, stop=True))
                p.vec(lambda e: e.tensor_copy(ytb[:], psY[:]))
                p.mm(lambda e: e.transpose(psTb[0:128, 0:64], ytb[:], idb[0:64, 0:64]))
                p.vec(lambda e, h=h: e.scalar_tensor_tensor(
                    ytile[:, h * 64:(h + 1) * 64], xnat[:], DbcD[:, h:h + 1],
                    psTb[0:128, 0:64], op0=ALU.mult, op1=ALU.add))
            p.act(lambda e: e.activation(sqscr[:], ytile[:], AF.Square))
            p.vec(lambda e: e.tensor_reduce(
                _r(ms8[:, :], "p g -> p g").unsqueeze(2).squeeze(2),
                _r(sqscr[:, :], "p (g c) -> p g c", g=8), axis=AX.X, op=ALU.add))
            p.vec(lambda e: e.tensor_scalar(ms8[:], ms8[:], 1.0 / 512, 1e-5, op0=ALU.mult, op1=ALU.add))
            p.act(lambda e: e.activation(rstd8[:], ms8[:], AF.Ln))
            p.act(lambda e: e.activation(ms8[:], rstd8[:], AF.Exp, scale=-0.5))
            for g in range(8):
                p.vec(lambda e, g=g: e.scalar_tensor_tensor(
                    ytile[:, g * 512:(g + 1) * 512], ytile[:, g * 512:(g + 1) * 512],
                    ms8[:, g:g + 1], nwbc[:, g * 512:(g + 1) * 512], op0=ALU.mult, op1=ALU.mult))
            for cc in range(32):
                p.mm(lambda e, cc=cc: e.transpose(psTb[:], gch[:, cc, :], idb[:]))
                p.vec(lambda e, cc=cc: e.tensor_copy(gnat[:, cc * 128:(cc + 1) * 128], psTb[:]))
            p.act(lambda e: e.activation(sqscr[:], gnat[:], AF.Silu))
            p.vec(lambda e: e.tensor_tensor(ytile[:], ytile[:], sqscr[:], op=ALU.mult))
            p.vec(lambda e: e.tensor_copy(ybf[:], ytile[:]))
            for cc in range(32):
                p.mm(lambda e, cc=cc: e.transpose(psTb[:], ybf[:, cc * 128:(cc + 1) * 128], idb[:]))
                p.vec(lambda e: e.tensor_copy(ytt[:], psTb[:]))
                p.dma(lambda e, cc=cc, k=k: e.dma_start(
                    ynt_o[cc * 128:(cc + 1) * 128, k * 128:(k + 1) * 128], ytt[:]))
        p.emit(nc, sem, blk)
    return nc


def build_c3():
    nc = bass.Bass("TRN2", target_bir_lowering=False)
    xn1t_s = nc.dram_tensor("xn1t_s", [DM, TC], BF16, kind="ExternalInput").ap()
    ynt_s = nc.dram_tensor("ynt_s", [DM, TC], BF16, kind="ExternalInput").ap()
    opT = nc.dram_tensor("opT", [DM, DM], BF16, kind="ExternalInput").ap()
    f1T = nc.dram_tensor("f1T", [DM, DM], BF16, kind="ExternalInput").ap()
    f1b = nc.dram_tensor("f1b", [1, DM], F32, kind="ExternalInput").ap()
    x1_s = nc.dram_tensor("x1_s", [TC, DM], F32, kind="ExternalInput").ap()

    out_o = nc.dram_tensor("out_o", [TC, DM], F32, kind="ExternalOutput").ap()

    p = Prog()
    with ExitStack() as es:
        al = Alloc(nc, es)
        xnT_ = al.sb("xnT", [128, 32 * TC], BF16)
        opw_ = al.sb("opw", [128, 32 * 512], BF16)
        f1w_ = al.sb("f1w", [128, 32 * 512], BF16)
        ynT_ = al.sb("ynT", [128, 32 * 128], BF16)
        x1scr = al.sb("x1scr", [128, 512], F32)
        osb = al.sb("osb", [128, 512], F32)
        rowf = al.sb("rowf", [1, DM], F32)
        f1bb = al.sb("f1bb", [1, DM], BF16)
        onesb = al.sb("onesb", [1, 512], BF16)
        psA = al.ps("psA", [128, 512], F32)
        psB2 = al.ps("psB2", [128, 512], F32)
        sem = al.sem("sem")
        blk = al.block()
        xnT = _r(xnT_[:, :], "p (cc t) -> p cc t", cc=32)
        opw = _r(opw_[:, :], "p (cc m) -> p cc m", cc=32)
        f1w = _r(f1w_[:, :], "p (cc m) -> p cc m", cc=32)
        ynT = _r(ynT_[:, :], "p (cc t) -> p cc t", cc=32)
        xnv = _r(xn1t_s, "(cc p) t -> p cc t", p=128)
        for q in range(4):
            p.dma(lambda e, q=q: e.dma_start(xnT[:, q * 8:(q + 1) * 8, :], xnv[:, q * 8:(q + 1) * 8, :]))
        p.dma(lambda e: e.dma_start(rowf[:], f1b[:]))
        p.vec(lambda e: e.tensor_copy(f1bb[:], rowf[:]))
        p.vec(lambda e: e.memset(onesb[:], 1.0))
        opv = _r(opT, "(cc p) m -> p cc m", p=128)
        f1v = _r(f1T, "(cc p) m -> p cc m", p=128)
        ynv = _r(ynt_s, "(cc p) t -> p cc t", p=128)
        for dmg in range(8):
            for q in range(2):
                p.dma(lambda e, dmg=dmg, q=q: e.dma_start(
                    opw[:, q * 16:(q + 1) * 16, :], opv[:, q * 16:(q + 1) * 16, dmg * 512:(dmg + 1) * 512]))
                p.dma(lambda e, dmg=dmg, q=q: e.dma_start(
                    f1w[:, q * 16:(q + 1) * 16, :], f1v[:, q * 16:(q + 1) * 16, dmg * 512:(dmg + 1) * 512]))
            for ti in range(NT):
                for q in range(2):
                    p.dma(lambda e, ti=ti, q=q: e.dma_start(
                        ynT[:, q * 16:(q + 1) * 16, :], ynv[:, q * 16:(q + 1) * 16, ti * 128:(ti + 1) * 128]))
                p.dma(lambda e, ti=ti, dmg=dmg: e.dma_start(
                    x1scr[:], x1_s[ti * 128:(ti + 1) * 128, dmg * 512:(dmg + 1) * 512]))
                for cc in range(32):
                    p.mm(lambda e, cc=cc: e.matmul(psA[:], ynT[:, cc, :], opw[:, cc, :],
                                                   start=(cc == 0), stop=(cc == 31)))
                p.mm(lambda e, dmg=dmg: e.matmul(psB2[:], onesb[0:1, 0:128],
                                                 f1bb[0:1, dmg * 512:(dmg + 1) * 512],
                                                 start=True, stop=False))
                for cc in range(32):
                    p.mm(lambda e, cc=cc, ti=ti: e.matmul(
                        psB2[:], xnT[:, cc, ti * 128:(ti + 1) * 128], f1w[:, cc, :],
                        start=False, stop=(cc == 31)))
                p.vec(lambda e: e.tensor_tensor(osb[:], psA[:], x1scr[:], op=ALU.add))
                p.vec(lambda e: e.tensor_tensor(osb[:], osb[:], psB2[:], op=ALU.add))
                p.dma(lambda e, ti=ti, dmg=dmg: e.dma_start(
                    out_o[ti * 128:(ti + 1) * 128, dmg * 512:(dmg + 1) * 512], osb[:]))
        p.emit(nc, sem, blk)
    return nc


def kernel(L0_attn_x, L0_attn_Wqkv, L0_attn_bqkv, L0_attn_Wo, L0_attn_bo,
           L0_ffn_W_res, L0_ffn_W_pre, L0_ffn_W_post, L0_ffn_H_res_base,
           L0_ffn_H_pre_base, L0_ffn_H_post_base, L0_ffn_gate, L0_ffn_W_branch,
           L1_attn_in_proj_w, L1_attn_conv_weight, L1_attn_conv_bias,
           L1_attn_dt_bias, L1_attn_A_log, L1_attn_D, L1_attn_norm_weight,
           L1_attn_out_proj_w, L1_ffn_qweight, L1_ffn_scales, L1_ffn_scaled_zeros,
           L1_ffn_bias):
    cores = list(range(NC))
    identf = np.eye(128, dtype=np.float32)
    identb = np.eye(128, dtype=BF)
    triu = np.triu(np.ones((128, 128), dtype=np.float32))

    def c(a):
        return np.ascontiguousarray(a)

    ip_pad = np.concatenate([L1_attn_in_proj_w,
                             np.zeros((11264 - PROJ, DM), np.float32)], axis=0)
    maps = []
    for i in cores:
        maps.append({
            "wqkv_s": c(L0_attn_Wqkv[i * 512:(i + 1) * 512]),
            "wo_s": c(L0_attn_Wo[i * 128:(i + 1) * 128]),
            "wbr_s": c(L0_ffn_W_branch[i * 128:(i + 1) * 128]),
            "ip_s": c(ip_pad[i * 1408:(i + 1) * 1408]),
            "op_s": c(L1_attn_out_proj_w[i * 512:(i + 1) * 512]),
            "qw_s": c(L1_ffn_qweight.reshape(DM, DM)[i * 512:(i + 1) * 512]),
            "sc_s": c(L1_ffn_scales.reshape(DM, 32)[i * 512:(i + 1) * 512]),
            "zp_s": c(L1_ffn_scaled_zeros.reshape(DM, 32)[i * 512:(i + 1) * 512]),
            "identb": identb,
        })
    r0 = run_bass_kernel_spmd(build_p0(), maps, core_ids=cores).results
    wqkv_b = np.concatenate([r["wqkv_o"] for r in r0], axis=0)
    wo_b = np.concatenate([r["wo_o"] for r in r0], axis=0)
    wbr_b = np.concatenate([r["wbr_o"] for r in r0], axis=0)
    ipT = np.concatenate([r["ipT_o"] for r in r0], axis=1)[:, 0:PROJ]
    opT = np.concatenate([r["opT_o"] for r in r0], axis=1)
    f1T = np.concatenate([r["f1T_o"] for r in r0], axis=1)

    x = L0_attn_x.reshape(S, DM)
    wcat = np.concatenate([L0_ffn_W_res, L0_ffn_W_pre, L0_ffn_W_post], axis=0)
    bases = np.concatenate([L0_ffn_H_res_base.reshape(-1), L0_ffn_H_pre_base,
                            L0_ffn_H_post_base])[None, :]
    maps = []
    for i in cores:
        maps.append({
            "x_s": c(x[i * TC:(i + 1) * TC]),
            "wqkv_b": wqkv_b, "bqkv": c(L0_attn_bqkv[None, :]),
            "wcat": c(wcat), "bases": c(bases),
            "gatein": c(L0_ffn_gate[None, :]),
            "eye16": c(np.eye(4, dtype=np.float32).reshape(1, 16)),
            "wbr_b": wbr_b, "identb": identb, "identf": identf,
        })
    ra = run_bass_kernel_spmd(build_a(), maps, core_ids=cores).results
    kt_f = np.concatenate([r["qkt_o"][1024:2048] for r in ra], axis=1)
    vn_f = np.concatenate([r["vn_o"] for r in ra], axis=0)

    maps = []
    for i in cores:
        maps.append({
            "qt_s": c(ra[i]["qkt_o"][0:1024]),
            "kt_f": kt_f, "vn_f": vn_f,
            "part_s": ra[i]["part_o"],
            "wo_b": wo_b, "bo": c(L0_attn_bo[None, :]), "identb": identb,
        })
    rb = run_bass_kernel_spmd(build_b(), maps, core_ids=cores).results

    maps = []
    for i in cores:
        maps.append({"xn1t_s": rb[i]["xn1t_o"], "ipT": ipT})
    rc1 = run_bass_kernel_spmd(build_c1(), maps, core_ids=cores).results

    hbct = np.concatenate([r["hbct_o"] for r in rc1], axis=1)
    maps = []
    for i in cores:
        if i == 0:
            halo = np.concatenate([np.zeros((CONV, 3), BF), hbct[:, 0:TC]], axis=1)
        else:
            halo = hbct[:, i * TC - 3:(i + 1) * TC]
        maps.append({
            "halo": c(halo),
            "cw_in": c(L1_attn_conv_weight.reshape(48, 128, 4).transpose(1, 0, 2).reshape(128, 192)),
            "cb_in": c(L1_attn_conv_bias.reshape(48, 128).T),
            "dtt_s": rc1[i]["dtt_o"],
            "dtb_in": c(L1_attn_dt_bias[:, None]),
            "alog_in": c(L1_attn_A_log[:, None]),
            "identb": identb, "identf": identf,
        })
    rc2a = run_bass_kernel_spmd(build_c2a(), maps, core_ids=cores).results

    mprimes = [r["mprime_o"] for r in rc2a]
    dtots = [r["cumdec_o"][:, 8] for r in rc2a]
    maps = []
    for i in cores:
        pm = np.zeros((7 * 128, DM), BF)
        pd = np.ones((7, 64), np.float32)
        nprev = i
        for j in range(nprev):
            s0 = 7 - nprev + j
            pm[s0 * 128:(s0 + 1) * 128] = mprimes[j]
            pd[s0] = dtots[j]
        cumd = rc2a[i]["cumdec_o"].T.reshape(1, 9 * 64)
        maps.append({
            "xbct_s": rc2a[i]["xbct_o"],
            "gatet_s": rc1[i]["gatet_o"],
            "snat_s": rc2a[i]["snat_o"],
            "dtpn_s": rc2a[i]["dtpn_o"],
            "sloc_s": rc2a[i]["sloc_o"],
            "prevM": pm, "prevD": c(pd.reshape(1, 448)),
            "cumd": c(cumd),
            "d_in": c(L1_attn_D[None, :]),
            "nw_in": c(L1_attn_norm_weight[None, :]),
            "triu_in": triu, "identb": identb, "identf": identf,
        })
    rc2b = run_bass_kernel_spmd(build_c2b(), maps, core_ids=cores).results

    maps = []
    for i in cores:
        maps.append({
            "xn1t_s": rb[i]["xn1t_o"], "ynt_s": rc2b[i]["ynt_o"],
            "opT": opT, "f1T": f1T,
            "f1b": c(L1_ffn_bias[None, :]),
            "x1_s": rb[i]["x1_o"],
        })
    rc3 = run_bass_kernel_spmd(build_c3(), maps, core_ids=cores).results
    out = np.concatenate([r["out_o"] for r in rc3], axis=0)
    return out.reshape(1, S, DM).astype(np.float32)


# version 03 (final)
# speedup vs baseline: 1.3328x; 1.3328x over previous
import sys
import numpy as np

sys.path.insert(0, '/opt/trn_rl_repo')
import concourse.bass as bass
import concourse.mybir as mybir
from concourse.bass_utils import run_bass_kernel_spmd
import ml_dtypes
from contextlib import ExitStack

F32 = mybir.dt.float32
BF16 = mybir.dt.bfloat16
I32 = mybir.dt.int32
AF = mybir.ActivationFunctionType
ALU = mybir.AluOpType
AX = mybir.AxisListType
BF = ml_dtypes.bfloat16

NC = 8
S = 8192
TC = 1024
NT = 8
DM = 4096
DQ = 1024
F0 = 3072
PROJ = 10304
CONV = 6144
H, PP, NN, G, KK = 64, 64, 128, 8, 4


class Prog:
    def __init__(self):
        self.ops = []

    def dma(self, fn):
        self.ops.append(("sync", fn, 16))

    def mm(self, fn):
        self.ops.append(("tensor", fn, 1))

    def vec(self, fn):
        self.ops.append(("vector", fn, 1))

    def act(self, fn):
        self.ops.append(("scalar", fn, 1))

    def emit(self, nc, sem, blk):
        groups = []
        for (e, fn, inc) in self.ops:
            if groups and groups[-1][0] == e and e in ("tensor", "sync"):
                groups[-1][1].append(fn)
                groups[-1][2] += inc
            else:
                groups.append([e, [fn], inc])
        pre = []
        c = 0
        for g in groups:
            pre.append(c)
            c += g[2]
        total = c

        def mk(name):
            def inner(eng):
                for gi, g in enumerate(groups):
                    if g[0] == name:
                        if pre[gi] > 0:
                            eng.wait_ge(sem, pre[gi])
                        if name == "sync":
                            for fn in g[1]:
                                fn(eng).then_inc(sem, 16)
                        else:
                            n = len(g[1])
                            for j, fn in enumerate(g[1]):
                                inst = fn(eng)
                                if j == n - 1:
                                    inst.then_inc(sem, g[2])
            return inner

        blk.sync(mk("sync"))
        blk.tensor(mk("tensor"))
        blk.vector(mk("vector"))
        blk.scalar(mk("scalar"))

        def gps(eng):
            mk("gpsimd")(eng)
            eng.wait_ge(sem, total)
        blk.gpsimd(gps)


class Alloc:
    def __init__(self, nc, es):
        self.nc, self.es = nc, es

    def sb(self, name, shape, dt):
        return self.es.enter_context(self.nc.sbuf_tensor(name, shape, dt))

    def ps(self, name, shape, dt):
        return self.es.enter_context(self.nc.psum_tensor(name, shape, dt))

    def sem(self, name):
        return self.es.enter_context(self.nc.semaphore(name))

    def block(self):
        return self.es.enter_context(self.nc.Block())


def _r(t, pat, **kw):
    return t.rearrange(pat, **kw)


def build_p0():
    nc = bass.Bass("TRN2", target_bir_lowering=False)
    wqkv_s = nc.dram_tensor("wqkv_s", [512, F0], F32, kind="ExternalInput").ap()
    wo_s = nc.dram_tensor("wo_s", [128, DM], F32, kind="ExternalInput").ap()
    wbr_s = nc.dram_tensor("wbr_s", [128, 1024], F32, kind="ExternalInput").ap()
    ip_s = nc.dram_tensor("ip_s", [1408, DM], F32, kind="ExternalInput").ap()
    op_s = nc.dram_tensor("op_s", [512, DM], F32, kind="ExternalInput").ap()
    qw_s = nc.dram_tensor("qw_s", [512, DM], I32, kind="ExternalInput").ap()
    sc_s = nc.dram_tensor("sc_s", [512, 32], F32, kind="ExternalInput").ap()
    zp_s = nc.dram_tensor("zp_s", [512, 32], F32, kind="ExternalInput").ap()
    identb = nc.dram_tensor("identb", [128, 128], BF16, kind="ExternalInput").ap()

    wqkv_o = nc.dram_tensor("wqkv_o", [512, F0], BF16, kind="ExternalOutput").ap()
    wo_o = nc.dram_tensor("wo_o", [128, DM], BF16, kind="ExternalOutput").ap()
    wbr_o = nc.dram_tensor("wbr_o", [128, 1024], BF16, kind="ExternalOutput").ap()
    ipT_o = nc.dram_tensor("ipT_o", [DM, 1408], BF16, kind="ExternalOutput").ap()
    opT_o = nc.dram_tensor("opT_o", [DM, 512], BF16, kind="ExternalOutput").ap()
    f1T_o = nc.dram_tensor("f1T_o", [DM, 512], BF16, kind="ExternalOutput").ap()

    p = Prog()
    with ExitStack() as es:
        al = Alloc(nc, es)
        tb = al.sb("tb", [128, DM], F32)
        tbb = al.sb("tbb", [128, DM], BF16)
        qb = al.sb("qb", [128, DM], I32)
        qf = al.sb("qf", [128, DM], F32)
        scb = al.sb("scb", [128, 32], F32)
        zpb = al.sb("zpb", [128, 32], F32)
        idb = al.sb("idb", [128, 128], BF16)
        ot = al.sb("ot", [128, 128], BF16)
        pstb = al.ps("pstb", [128, 128], BF16)
        sem = al.sem("sem")
        blk = al.block()
        p.dma(lambda e: e.dma_start(idb[:], identb[:]))
        for b in range(4):
            p.dma(lambda e, b=b: e.dma_start(tb[:, 0:F0], wqkv_s[b * 128:(b + 1) * 128, :]))
            p.vec(lambda e: e.tensor_copy(tbb[:, 0:F0], tb[:, 0:F0]))
            p.dma(lambda e, b=b: e.dma_start(wqkv_o[b * 128:(b + 1) * 128, :], tbb[:, 0:F0]))
        p.dma(lambda e: e.dma_start(tb[:], wo_s[:]))
        p.vec(lambda e: e.tensor_copy(tbb[:], tb[:]))
        p.dma(lambda e: e.dma_start(wo_o[:], tbb[:]))
        p.dma(lambda e: e.dma_start(tb[:, 0:1024], wbr_s[:]))
        p.vec(lambda e: e.tensor_copy(tbb[:, 0:1024], tb[:, 0:1024]))
        p.dma(lambda e: e.dma_start(wbr_o[:], tbb[:, 0:1024]))

        def transpose_out(src_ap, out_hbm, b):
            for cc in range(32):
                p.mm(lambda e, cc=cc: e.transpose(pstb[:], src_ap[:, cc * 128:(cc + 1) * 128], idb[:]))
                p.vec(lambda e: e.tensor_copy(ot[:], pstb[:]))
                p.dma(lambda e, cc=cc, b=b: e.dma_start(
                    out_hbm[cc * 128:(cc + 1) * 128, b * 128:(b + 1) * 128], ot[:]))

        for b in range(11):
            p.dma(lambda e, b=b: e.dma_start(tb[:], ip_s[b * 128:(b + 1) * 128, :]))
            p.vec(lambda e: e.tensor_copy(tbb[:], tb[:]))
            transpose_out(tbb, ipT_o, b)
        for b in range(4):
            p.dma(lambda e, b=b: e.dma_start(tb[:], op_s[b * 128:(b + 1) * 128, :]))
            p.vec(lambda e: e.tensor_copy(tbb[:], tb[:]))
            transpose_out(tbb, opT_o, b)
        for b in range(4):
            p.dma(lambda e, b=b: e.dma_start(qb[:], qw_s[b * 128:(b + 1) * 128, :]))
            p.dma(lambda e, b=b: e.dma_start(scb[:], sc_s[b * 128:(b + 1) * 128, :]))
            p.dma(lambda e, b=b: e.dma_start(zpb[:], zp_s[b * 128:(b + 1) * 128, :]))
            p.vec(lambda e: e.tensor_copy(qf[:], qb[:]))
            for g in range(32):
                p.vec(lambda e, g=g: e.tensor_scalar(
                    tbb[:, g * 128:(g + 1) * 128], qf[:, g * 128:(g + 1) * 128],
                    scb[:, g:g + 1], zpb[:, g:g + 1], op0=ALU.mult, op1=ALU.add))
            transpose_out(tbb, f1T_o, b)
        p.emit(nc, sem, blk)
    return nc


def build_a():
    nc = bass.Bass("TRN2", target_bir_lowering=False)
    x_s = nc.dram_tensor("x_s", [TC, DM], F32, kind="ExternalInput").ap()
    wqkv_b = nc.dram_tensor("wqkv_b", [DM, F0], BF16, kind="ExternalInput").ap()
    bqkv = nc.dram_tensor("bqkv", [1, F0], F32, kind="ExternalInput").ap()
    wcat = nc.dram_tensor("wcat", [24, DM], F32, kind="ExternalInput").ap()
    bases = nc.dram_tensor("bases", [1, 24], F32, kind="ExternalInput").ap()
    gatein = nc.dram_tensor("gatein", [1, 1], F32, kind="ExternalInput").ap()
    eye16 = nc.dram_tensor("eye16", [1, 16], F32, kind="ExternalInput").ap()
    wbr_b = nc.dram_tensor("wbr_b", [1024, 1024], BF16, kind="ExternalInput").ap()
    identb = nc.dram_tensor("identb", [128, 128], BF16, kind="ExternalInput").ap()
    identf = nc.dram_tensor("identf", [128, 128], F32, kind="ExternalInput").ap()

    qkt_o = nc.dram_tensor("qkt_o", [2048, TC], BF16, kind="ExternalOutput").ap()
    vn_o = nc.dram_tensor("vn_o", [TC, DQ], BF16, kind="ExternalOutput").ap()
    part_o = nc.dram_tensor("part_o", [TC, DM], F32, kind="ExternalOutput").ap()

    p = Prog()
    with ExitStack() as es:
        al = Alloc(nc, es)
        xs = al.sb("xs", [128, DM], F32)
        sq = al.sb("sq", [128, DM], F32)
        xnT_ = al.sb("xnT", [128, 32 * TC], BF16)
        mcol = al.sb("mcol", [128, 8], F32)
        rstd = al.sb("rstd", [128, 8], F32)
        rmsr = al.sb("rmsr", [128, 8], F32)
        tcol = al.sb("tcol", [128, 1], F32)
        t2col = al.sb("t2col", [128, 1], F32)
        wbuf_ = al.sb("wbuf", [128, 32 * 128], BF16)
        qksb = al.sb("qksb", [128, 512], BF16)
        vtile = al.sb("vtile", [128, 128], BF16)
        onesf = al.sb("onesf", [1, 128], F32)
        onesb = al.sb("onesb", [1, 512], BF16)
        bq_b = al.sb("bq_b", [1, F0], BF16)
        rowf = al.sb("rowf", [1, F0], F32)
        wcat_f = al.sb("wcat_f", [24, DM], F32)
        wcat_b = al.sb("wcat_b", [24, DM], BF16)
        wcatT = al.sb("wcatT", [128, 32 * 24], BF16)
        logitsT = al.sb("logitsT", [24, TC], F32)
        lgn = al.sb("lgn", [128, 24], F32)
        base_bc = al.sb("base_bc", [128, 24], F32)
        eye_bc = al.sb("eye_bc", [128, 16], F32)
        Zb = al.sb("Zb", [128, 128], F32)
        A4 = al.sb("A4", [128, 128], F32)
        E4 = al.sb("E4", [128, 128], F32)
        u4 = al.sb("u4", [128, 32], F32)
        v4 = al.sb("v4", [128, 32], F32)
        m4 = al.sb("m4", [128, 32], F32)
        s4 = al.sb("s4", [128, 32], F32)
        l4 = al.sb("l4", [128, 32], F32)
        hpre = al.sb("hpre", [128, 32], F32)
        hpost = al.sb("hpost", [128, 32], F32)
        hscr = al.sb("hscr", [128, 32], F32)
        gcol = al.sb("gcol", [128, 1], F32)
        gm1 = al.sb("gm1", [128, 1], F32)
        xp = al.sb("xp", [128, 1024], F32)
        xpb = al.sb("xpb", [128, 1024], BF16)
        xpT = al.sb("xpT", [128, 1024], BF16)
        ysb = al.sb("ysb", [128, 1024], F32)
        wbr_sb_ = al.sb("wbr_sb", [128, 8 * 1024], BF16)
        pt = al.sb("pt", [128, DM], F32)
        idb = al.sb("idb", [128, 128], BF16)
        idf = al.sb("idf", [128, 128], F32)
        psQ = al.ps("psQ", [128, 512], F32)
        psTf = al.ps("psTf", [128, 128], F32)
        psTb = al.ps("psTb", [128, 128], BF16)
        sem = al.sem("sem")
        blk = al.block()
        xnT = _r(xnT_[:, :], "p (cc t) -> p cc t", cc=32)
        wbuf = _r(wbuf_[:, :], "p (cc f) -> p cc f", cc=32)
        wbr_sb = _r(wbr_sb_[:, :], "p (cc f) -> p cc f", cc=8)

        p.dma(lambda e: e.dma_start(idb[:], identb[:]))
        p.dma(lambda e: e.dma_start(idf[:], identf[:]))
        p.vec(lambda e: e.memset(onesf[:], 1.0))
        p.vec(lambda e: e.memset(onesb[:], 1.0))

        for i in range(NT):
            p.dma(lambda e, i=i: e.dma_start(xs[:], x_s[i * 128:(i + 1) * 128, :]))
            p.vec(lambda e: e.reduce_sum(tcol[:], xs[:], axis=AX.X))
            p.vec(lambda e, i=i: e.tensor_scalar(mcol[:, i:i + 1], tcol[:], 1.0 / DM, None, op0=ALU.mult))
            p.vec(lambda e, i=i: e.tensor_scalar(xs[:], xs[:], mcol[:, i:i + 1], None, op0=ALU.subtract))
            p.act(lambda e: e.activation(sq[:], xs[:], AF.Square, accum_out=tcol[:]))
            p.vec(lambda e: e.tensor_scalar(t2col[:], tcol[:], 1.0 / DM, 1e-5, op0=ALU.mult, op1=ALU.add))
            p.act(lambda e: e.activation(tcol[:], t2col[:], AF.Ln))
            p.act(lambda e, i=i: e.activation(rstd[:, i:i + 1], tcol[:], AF.Exp, scale=-0.5))
            p.vec(lambda e, i=i: e.tensor_scalar(xs[:], xs[:], rstd[:, i:i + 1], None, op0=ALU.mult))
            p.act(lambda e: e.activation(sq[:], xs[:], AF.Square, accum_out=tcol[:]))
            p.vec(lambda e: e.tensor_scalar(t2col[:], tcol[:], 1.0 / DM, 1.1920929e-07, op0=ALU.mult, op1=ALU.add))
            p.act(lambda e: e.activation(tcol[:], t2col[:], AF.Ln))
            p.act(lambda e, i=i: e.activation(rmsr[:, i:i + 1], tcol[:], AF.Exp, scale=-0.5))
            for cc in range(32):
                p.mm(lambda e, cc=cc: e.transpose(psTf[:], xs[:, cc * 128:(cc + 1) * 128], idf[:]))
                p.vec(lambda e, cc=cc, i=i: e.tensor_copy(xnT[:, cc, i * 128:(i + 1) * 128], psTf[:]))

        p.dma(lambda e: e.dma_start(wcat_f[:], wcat[:]))
        p.vec(lambda e: e.tensor_copy(wcat_b[:], wcat_f[:]))
        for cc in range(32):
            p.mm(lambda e, cc=cc: e.transpose(psTb[0:128, 0:24], wcat_b[:, cc * 128:(cc + 1) * 128], idb[0:24, 0:24]))
            p.vec(lambda e, cc=cc: e.tensor_copy(wcatT[:, cc * 24:(cc + 1) * 24], psTb[0:128, 0:24]))
        p.dma(lambda e: e.dma_start(rowf[0:1, 0:24], bases[:]))
        p.mm(lambda e: e.matmul(psTf[:, 0:24], onesf[:], rowf[0:1, 0:24], start=True, stop=True))
        p.vec(lambda e: e.tensor_copy(base_bc[:], psTf[:, 0:24]))
        p.dma(lambda e: e.dma_start(rowf[0:1, 0:16], eye16[:]))
        p.mm(lambda e: e.matmul(psTf[:, 0:16], onesf[:], rowf[0:1, 0:16], start=True, stop=True))
        p.vec(lambda e: e.tensor_copy(eye_bc[:], psTf[:, 0:16]))
        for tg in range(2):
            for cc in range(32):
                p.mm(lambda e, cc=cc, tg=tg: e.matmul(
                    psQ[0:24, :], wcatT[:, cc * 24:(cc + 1) * 24],
                    xnT[:, cc, tg * 512:(tg + 1) * 512], start=(cc == 0), stop=(cc == 31)))
            p.vec(lambda e, tg=tg: e.tensor_copy(logitsT[:, tg * 512:(tg + 1) * 512], psQ[0:24, :]))
        for i in range(NT):
            p.mm(lambda e, i=i: e.transpose(psTf[:, 0:24], logitsT[:, i * 128:(i + 1) * 128], idf[0:24, 0:24]))
            p.vec(lambda e, i=i: e.scalar_tensor_tensor(
                lgn[:], psTf[:, 0:24], rmsr[:, i:i + 1], base_bc[:], op0=ALU.mult, op1=ALU.add))
            p.vec(lambda e, i=i: e.tensor_scalar(Zb[:, i * 16:(i + 1) * 16], lgn[:, 0:16], 10.0, None, op0=ALU.mult))
            p.vec(lambda e, i=i: e.tensor_copy(hpre[:, i * 4:(i + 1) * 4], lgn[:, 16:20]))
            p.vec(lambda e, i=i: e.tensor_copy(hpost[:, i * 4:(i + 1) * 4], lgn[:, 20:24]))

        Zv = _r(Zb[:, :], "p (s i j) -> p s i j", i=4, j=4)
        A4v = _r(A4[:, :], "p (s i j) -> p s i j", i=4, j=4)
        A4ji = _r(A4[:, :], "p (s i j) -> p s j i", i=4, j=4)
        E4v = _r(E4[:, :], "p (s i j) -> p s i j", i=4, j=4)
        E4ji = _r(E4[:, :], "p (s i j) -> p s j i", i=4, j=4)
        m4i = _r(m4[:, :], "p (s i) -> p s i", i=4)
        s4i = _r(s4[:, :], "p (s i) -> p s i", i=4)
        u4i = _r(u4[:, :], "p (s i) -> p s i", i=4)
        v4j = _r(v4[:, :], "p (s j) -> p s j", j=4)
        vb = v4j.unsqueeze(2).broadcast_to((128, 8, 4, 4))
        ub = u4i.unsqueeze(3).broadcast_to((128, 8, 4, 4))
        mb = m4i.unsqueeze(3).broadcast_to((128, 8, 4, 4))

        p.vec(lambda e: e.memset(v4[:], 0.0))
        for it in range(50):
            p.vec(lambda e: e.tensor_tensor(A4v, Zv, vb, op=ALU.add))
            p.vec(lambda e: e.reduce_max(m4i, A4v, axis=AX.X))
            p.vec(lambda e: e.tensor_tensor(A4v, A4v, mb, op=ALU.subtract))
            p.act(lambda e: e.activation(E4[:], A4[:], AF.Exp))
            p.vec(lambda e: e.tensor_reduce(s4i, E4v, axis=AX.X, op=ALU.add))
            p.act(lambda e: e.activation(l4[:], s4[:], AF.Ln))
            p.vec(lambda e: e.tensor_tensor(l4[:], l4[:], m4[:], op=ALU.add))
            p.vec(lambda e: e.tensor_scalar(u4[:], l4[:], -1.0, None, op0=ALU.mult))
            p.vec(lambda e: e.tensor_tensor(A4v, Zv, ub, op=ALU.add))
            p.vec(lambda e: e.reduce_max(m4i, A4ji, axis=AX.X))
            p.vec(lambda e: e.tensor_tensor(A4ji, A4ji, mb, op=ALU.subtract))
            p.act(lambda e: e.activation(E4[:], A4[:], AF.Exp))
            p.vec(lambda e: e.tensor_reduce(s4i, E4ji, axis=AX.X, op=ALU.add))
            p.act(lambda e: e.activation(l4[:], s4[:], AF.Ln))
            p.vec(lambda e: e.tensor_tensor(l4[:], l4[:], m4[:], op=ALU.add))
            p.vec(lambda e: e.tensor_scalar(v4[:], l4[:], -1.0, None, op0=ALU.mult))
        p.vec(lambda e: e.tensor_tensor(A4v, Zv, ub, op=ALU.add))
        p.vec(lambda e: e.tensor_tensor(A4v, A4v, vb, op=ALU.add))
        p.act(lambda e: e.activation(Zb[:], A4[:], AF.Exp))
        p.dma(lambda e: e.dma_start(rowf[0:1, 0:1], gatein[:]))
        p.mm(lambda e: e.matmul(psTf[:, 0:1], onesf[:], rowf[0:1, 0:1], start=True, stop=True))
        p.act(lambda e: e.activation(gcol[:], psTf[:, 0:1], AF.Sigmoid))
        p.act(lambda e: e.activation(gm1[:], gcol[:], AF.Identity, scale=-1.0, bias=1.0))
        p.vec(lambda e: e.tensor_scalar(Zb[:], Zb[:], gcol[:], None, op0=ALU.mult))
        Zs16 = _r(Zb[:, :], "p (s k) -> p s k", k=16)
        eyeb = eye_bc[:, :].unsqueeze(1).broadcast_to((128, 8, 16))
        p.vec(lambda e: e.scalar_tensor_tensor(Zs16, eyeb, gm1[:], Zs16, op0=ALU.mult, op1=ALU.add))
        for hl in (hpre, hpost):
            hv = _r(hl[:, :], "p (s j) -> p s j", j=4)
            hm = _r(hscr[:, :], "p (s j) -> p s j", j=4)
            mb8 = m4[:, 0:8].unsqueeze(2).broadcast_to((128, 8, 4))
            lb8 = l4[:, 0:8].unsqueeze(2).broadcast_to((128, 8, 4))
            p.vec(lambda e, hv=hv: e.reduce_max(m4[:, 0:8], hv, axis=AX.X))
            p.vec(lambda e, hv=hv, mb8=mb8: e.tensor_tensor(hv, hv, mb8, op=ALU.subtract))
            p.act(lambda e, hl=hl: e.activation(hscr[:], hl[:], AF.Exp))
            p.vec(lambda e, hm=hm: e.tensor_reduce(s4[:, 0:8], hm, axis=AX.X, op=ALU.add))
            p.vec(lambda e: e.reciprocal(l4[:, 0:8], s4[:, 0:8]))
            p.vec(lambda e, hv=hv, hm=hm, lb8=lb8: e.tensor_tensor(hv, hm, lb8, op=ALU.mult))

        p.dma(lambda e: e.dma_start(rowf[:], bqkv[:]))
        p.vec(lambda e: e.tensor_copy(bq_b[:], rowf[:]))
        wq_view = _r(wqkv_b, "(cc p) f -> p cc f", p=128)
        for f in range(24):
            p.dma(lambda e, f=f: e.dma_start(wbuf[:, 0:16, :], wq_view[:, 0:16, f * 128:(f + 1) * 128]))
            p.dma(lambda e, f=f: e.dma_start(wbuf[:, 16:32, :], wq_view[:, 16:32, f * 128:(f + 1) * 128]))
            for tg in range(2):
                p.mm(lambda e, f=f: e.matmul(psQ[:], bq_b[0:1, f * 128:(f + 1) * 128], onesb[:], start=True, stop=False))
                for cc in range(32):
                    p.mm(lambda e, cc=cc, tg=tg: e.matmul(
                        psQ[:], wbuf[:, cc, :], xnT[:, cc, tg * 512:(tg + 1) * 512],
                        start=False, stop=(cc == 31)))
                if f < 16:
                    p.vec(lambda e: e.tensor_copy(qksb[:], psQ[:]))
                    p.dma(lambda e, f=f, tg=tg: e.dma_start(
                        qkt_o[f * 128:(f + 1) * 128, tg * 512:(tg + 1) * 512], qksb[:]))
                else:
                    p.vec(lambda e: e.tensor_copy(qksb[:], psQ[:]))
                    for j in range(4):
                        p.mm(lambda e, j=j: e.transpose(psTb[:], qksb[:, j * 128:(j + 1) * 128], idb[:]))
                        p.vec(lambda e: e.tensor_copy(vtile[:], psTb[:]))
                        p.dma(lambda e, f=f, tg=tg, j=j: e.dma_start(
                            vn_o[tg * 512 + j * 128: tg * 512 + (j + 1) * 128,
                                 (f - 16) * 128:(f - 15) * 128], vtile[:]))

        wbrv = _r(wbr_b, "(cc p) f -> p cc f", p=128)
        p.dma(lambda e: e.dma_start(wbr_sb[:, :, :], wbrv[:, :, :]))
        for i in range(NT):
            p.dma(lambda e, i=i: e.dma_start(xs[:], x_s[i * 128:(i + 1) * 128, :]))
            p.vec(lambda e, i=i: e.tensor_scalar(sq[:], xs[:], mcol[:, i:i + 1], None, op0=ALU.subtract))
            p.vec(lambda e, i=i: e.tensor_scalar(sq[:], sq[:], rstd[:, i:i + 1], None, op0=ALU.mult))
            p.vec(lambda e, i=i: e.tensor_scalar(xp[:], sq[:, 0:1024], hpre[:, i * 4:i * 4 + 1], None, op0=ALU.mult))
            for n in range(1, 4):
                p.vec(lambda e, i=i, n=n: e.scalar_tensor_tensor(
                    xp[:], sq[:, n * 1024:(n + 1) * 1024], hpre[:, i * 4 + n:i * 4 + n + 1],
                    xp[:], op0=ALU.mult, op1=ALU.add))
            p.vec(lambda e: e.tensor_copy(xpb[:], xp[:]))
            for cc in range(8):
                p.mm(lambda e, cc=cc: e.transpose(psTb[:], xpb[:, cc * 128:(cc + 1) * 128], idb[:]))
                p.vec(lambda e, cc=cc: e.tensor_copy(xpT[:, cc * 128:(cc + 1) * 128], psTb[:]))
            for co in range(2):
                for cc in range(8):
                    p.mm(lambda e, cc=cc, co=co: e.matmul(
                        psQ[:], xpT[:, cc * 128:(cc + 1) * 128], wbr_sb[:, cc, co * 512:(co + 1) * 512],
                        start=(cc == 0), stop=(cc == 7)))
                p.vec(lambda e, co=co: e.tensor_copy(ysb[:, co * 512:(co + 1) * 512], psQ[:]))
            for n in range(4):
                p.vec(lambda e, i=i, n=n: e.scalar_tensor_tensor(
                    pt[:, n * 1024:(n + 1) * 1024], sq[:, 0:1024],
                    Zb[:, i * 16 + n * 4:i * 16 + n * 4 + 1], xs[:, n * 1024:(n + 1) * 1024],
                    op0=ALU.mult, op1=ALU.add))
                for j in range(1, 4):
                    p.vec(lambda e, i=i, n=n, j=j: e.scalar_tensor_tensor(
                        pt[:, n * 1024:(n + 1) * 1024], sq[:, j * 1024:(j + 1) * 1024],
                        Zb[:, i * 16 + n * 4 + j:i * 16 + n * 4 + j + 1],
                        pt[:, n * 1024:(n + 1) * 1024], op0=ALU.mult, op1=ALU.add))
                p.vec(lambda e, i=i, n=n: e.scalar_tensor_tensor(
                    pt[:, n * 1024:(n + 1) * 1024], ysb[:],
                    hpost[:, i * 4 + n:i * 4 + n + 1], pt[:, n * 1024:(n + 1) * 1024],
                    op0=ALU.mult, op1=ALU.add))
            p.dma(lambda e, i=i: e.dma_start(part_o[i * 128:(i + 1) * 128, :], pt[:]))
        p.emit(nc, sem, blk)
    return nc


def build_b():
    nc = bass.Bass("TRN2", target_bir_lowering=False)
    qt_s = nc.dram_tensor("qt_s", [DQ, TC], BF16, kind="ExternalInput").ap()
    kt_f = nc.dram_tensor("kt_f", [DQ, S], BF16, kind="ExternalInput").ap()
    vn_f = nc.dram_tensor("vn_f", [S, DQ], BF16, kind="ExternalInput").ap()
    part_s = nc.dram_tensor("part_s", [TC, DM], F32, kind="ExternalInput").ap()
    wo_b = nc.dram_tensor("wo_b", [DQ, DM], BF16, kind="ExternalInput").ap()
    bo = nc.dram_tensor("bo", [1, DM], F32, kind="ExternalInput").ap()
    identb = nc.dram_tensor("identb", [128, 128], BF16, kind="ExternalInput").ap()

    x1_o = nc.dram_tensor("x1_o", [TC, DM], F32, kind="ExternalOutput").ap()
    xn1t_o = nc.dram_tensor("xn1t_o", [DM, TC], BF16, kind="ExternalOutput").ap()

    p = Prog()
    with ExitStack() as es:
        al = Alloc(nc, es)
        qt_ = al.sb("qt", [128, 8 * TC], BF16)
        ktb_ = al.sb("ktb", [128, 8 * 512], BF16)
        vb = al.sb("vb", [128, DQ], BF16)
        Sf = al.sb("Sf", [128, S], F32)
        Pf = al.sb("Pf", [128, S], F32)
        ones1b = al.sb("ones1b", [1, 128], BF16)
        negm = al.sb("negm", [128, 1], F32)
        sumc = al.sb("sumc", [128, 1], F32)
        rden = al.sb("rden", [128, 1], F32)
        pts = al.sb("pts", [128, 128], BF16)
        osb = al.sb("osb", [128, DQ], BF16)
        otT_ = al.sb("otT", [128, 8 * 128], BF16)
        wob_ = al.sb("wob", [128, 8 * 512], BF16)
        bo_b = al.sb("bo_b", [1, DM], BF16)
        rowf = al.sb("rowf", [1, DM], F32)
        onesb = al.sb("onesb", [1, 512], BF16)
        partt = al.sb("partt", [128, DM], F32)
        x1sb = al.sb("x1sb", [128, DM], F32)
        sq = al.sb("sq", [128, DM], F32)
        xnb = al.sb("xnb", [128, DM], BF16)
        tcol = al.sb("tcol", [128, 1], F32)
        t2col = al.sb("t2col", [128, 1], F32)
        xtile = al.sb("xtile", [128, 128], BF16)
        idb = al.sb("idb", [128, 128], BF16)
        psS = al.ps("psS", [128, 512], F32)
        psO0 = al.ps("psO0", [128, 512], F32)
        psO1 = al.ps("psO1", [128, 512], F32)
        psTb = al.ps("psTb", [128, 128], BF16)
        sem = al.sem("sem")
        blk = al.block()
        qt = _r(qt_[:, :], "p (dc t) -> p dc t", dc=8)
        ktb = _r(ktb_[:, :], "p (dc k) -> p dc k", dc=8)
        otT = _r(otT_[:, :], "p (dc q) -> p dc q", dc=8)
        wob = _r(wob_[:, :], "p (dc c) -> p dc c", dc=8)
        kt_v = _r(kt_f, "(dc p) k -> p dc k", p=128)
        wo_v = _r(wo_b, "(dc p) c -> p dc c", p=128)

        p.dma(lambda e: e.dma_start(idb[:], identb[:]))
        p.vec(lambda e: e.memset(onesb[:], 1.0))
        p.vec(lambda e: e.memset(ones1b[:], 1.0))
        p.dma(lambda e: e.dma_start(rowf[:], bo[:]))
        p.vec(lambda e: e.tensor_copy(bo_b[:], rowf[:]))
        p.dma(lambda e: e.dma_start(qt[:, :, :], _r(qt_s, "(dc p) t -> p dc t", p=128)[:, :, :]))

        for qi in range(NT):
            for g in range(16):
                p.dma(lambda e, g=g: e.dma_start(ktb[:, :, :], kt_v[:, :, g * 512:(g + 1) * 512]))
                for dc in range(8):
                    p.mm(lambda e, dc=dc, qi=qi: e.matmul(
                        psS[:], qt[:, dc, qi * 128:(qi + 1) * 128], ktb[:, dc, :],
                        start=(dc == 0), stop=(dc == 7)))
                p.vec(lambda e, g=g: e.tensor_scalar(
                    Sf[:, g * 512:(g + 1) * 512], psS[:], 0.03125, None, op0=ALU.mult))
            p.vec(lambda e: e.reduce_max(negm[:], Sf[:], axis=AX.X, negate=True))
            p.act(lambda e: e.activation(Pf[:], Sf[:], AF.Exp, bias=negm[:], accum_out=sumc[:]))
            p.vec(lambda e: e.reciprocal(rden[:], sumc[:]))
            for kv in range(64):
                p.dma(lambda e, kv=kv: e.dma_start(vb[:], vn_f[kv * 128:(kv + 1) * 128, :]))
                p.vec(lambda e, kv=kv: e.tensor_copy(pts[:], Pf[:, kv * 128:(kv + 1) * 128]))
                p.mm(lambda e: e.transpose(psTb[:], pts[:], idb[:]))
                p.vec(lambda e: e.tensor_copy(pts[:], psTb[:]))
                p.mm(lambda e, kv=kv: e.matmul(psO0[:], pts[:], vb[:, 0:512],
                                               start=(kv == 0), stop=(kv == 63)))
                p.mm(lambda e, kv=kv: e.matmul(psO1[:], pts[:], vb[:, 512:1024],
                                               start=(kv == 0), stop=(kv == 63)))
            p.vec(lambda e: e.tensor_scalar(osb[:, 0:512], psO0[:], rden[:], None, op0=ALU.mult))
            p.vec(lambda e: e.tensor_scalar(osb[:, 512:1024], psO1[:], rden[:], None, op0=ALU.mult))
            for dc in range(8):
                p.mm(lambda e, dc=dc: e.transpose(psTb[:], osb[:, dc * 128:(dc + 1) * 128], idb[:]))
                p.vec(lambda e, dc=dc: e.tensor_copy(otT[:, dc, :], psTb[:]))
            p.dma(lambda e, qi=qi: e.dma_start(partt[:], part_s[qi * 128:(qi + 1) * 128, :]))
            for cg in range(8):
                p.dma(lambda e, cg=cg: e.dma_start(wob[:, :, :], wo_v[:, :, cg * 512:(cg + 1) * 512]))
                p.mm(lambda e, cg=cg: e.matmul(psS[:], ones1b[:], bo_b[0:1, cg * 512:(cg + 1) * 512],
                                               start=True, stop=False))
                for dc in range(8):
                    p.mm(lambda e, dc=dc: e.matmul(psS[:], otT[:, dc, :], wob[:, dc, :],
                                                   start=False, stop=(dc == 7)))
                p.vec(lambda e, cg=cg: e.tensor_tensor(
                    x1sb[:, cg * 512:(cg + 1) * 512], psS[:], partt[:, cg * 512:(cg + 1) * 512], op=ALU.add))
            p.dma(lambda e, qi=qi: e.dma_start(x1_o[qi * 128:(qi + 1) * 128, :], x1sb[:]))
            p.vec(lambda e: e.reduce_sum(tcol[:], x1sb[:], axis=AX.X))
            p.vec(lambda e: e.tensor_scalar(tcol[:], tcol[:], 1.0 / DM, None, op0=ALU.mult))
            p.vec(lambda e: e.tensor_scalar(x1sb[:], x1sb[:], tcol[:], None, op0=ALU.subtract))
            p.act(lambda e: e.activation(sq[:], x1sb[:], AF.Square, accum_out=tcol[:]))
            p.vec(lambda e: e.tensor_scalar(t2col[:], tcol[:], 1.0 / DM, 1e-5, op0=ALU.mult, op1=ALU.add))
            p.act(lambda e: e.activation(tcol[:], t2col[:], AF.Ln))
            p.act(lambda e: e.activation(t2col[:], tcol[:], AF.Exp, scale=-0.5))
            p.vec(lambda e: e.tensor_scalar(x1sb[:], x1sb[:], t2col[:], None, op0=ALU.mult))
            p.vec(lambda e: e.tensor_copy(xnb[:], x1sb[:]))
            for cc in range(32):
                p.mm(lambda e, cc=cc: e.transpose(psTb[:], xnb[:, cc * 128:(cc + 1) * 128], idb[:]))
                p.vec(lambda e: e.tensor_copy(xtile[:], psTb[:]))
                p.dma(lambda e, cc=cc, qi=qi: e.dma_start(
                    xn1t_o[cc * 128:(cc + 1) * 128, qi * 128:(qi + 1) * 128], xtile[:]))
        p.emit(nc, sem, blk)
    return nc


def build_c1():
    nc = bass.Bass("TRN2", target_bir_lowering=False)
    xn1t_s = nc.dram_tensor("xn1t_s", [DM, TC], BF16, kind="ExternalInput").ap()
    ipT = nc.dram_tensor("ipT", [DM, PROJ], BF16, kind="ExternalInput").ap()

    gatet_o = nc.dram_tensor("gatet_o", [DM, TC], BF16, kind="ExternalOutput").ap()
    hbct_o = nc.dram_tensor("hbct_o", [CONV, TC], BF16, kind="ExternalOutput").ap()
    dtt_o = nc.dram_tensor("dtt_o", [64, TC], F32, kind="ExternalOutput").ap()

    p = Prog()
    with ExitStack() as es:
        al = Alloc(nc, es)
        xnT_ = al.sb("xnT", [128, 32 * TC], BF16)
        ipw_ = al.sb("ipw", [128, 32 * 128], BF16)
        evb = al.sb("evb", [128, 512], BF16)
        evf = al.sb("evf", [64, 512], F32)
        psP = al.ps("psP", [128, 512], F32)
        sem = al.sem("sem")
        blk = al.block()
        xnT = _r(xnT_[:, :], "p (cc t) -> p cc t", cc=32)
        ipw = _r(ipw_[:, :], "p (cc m) -> p cc m", cc=32)
        xv = _r(xn1t_s, "(cc p) t -> p cc t", p=128)
        ipv = _r(ipT, "(cc p) m -> p cc m", p=128)
        for q in range(4):
            p.dma(lambda e, q=q: e.dma_start(xnT[:, q * 8:(q + 1) * 8, :], xv[:, q * 8:(q + 1) * 8, :]))
        for pb in range(81):
            m = 128 if pb < 80 else 64
            p.dma(lambda e, pb=pb, m=m: e.dma_start(ipw[:, 0:16, 0:m], ipv[:, 0:16, pb * 128:pb * 128 + m]))
            p.dma(lambda e, pb=pb, m=m: e.dma_start(ipw[:, 16:32, 0:m], ipv[:, 16:32, pb * 128:pb * 128 + m]))
            for tg in range(2):
                for cc in range(32):
                    p.mm(lambda e, cc=cc, tg=tg, m=m: e.matmul(
                        psP[0:m, :], ipw[:, cc, 0:m], xnT[:, cc, tg * 512:(tg + 1) * 512],
                        start=(cc == 0), stop=(cc == 31)))
                if pb < 32:
                    p.vec(lambda e, m=m: e.tensor_copy(evb[0:m, :], psP[0:m, :]))
                    p.dma(lambda e, pb=pb, tg=tg: e.dma_start(
                        gatet_o[pb * 128:(pb + 1) * 128, tg * 512:(tg + 1) * 512], evb[:]))
                elif pb < 80:
                    p.vec(lambda e, m=m: e.tensor_copy(evb[0:m, :], psP[0:m, :]))
                    p.dma(lambda e, pb=pb, tg=tg: e.dma_start(
                        hbct_o[(pb - 32) * 128:(pb - 31) * 128, tg * 512:(tg + 1) * 512], evb[:]))
                else:
                    p.vec(lambda e: e.tensor_copy(evf[:], psP[0:64, :]))
                    p.dma(lambda e, tg=tg: e.dma_start(
                        dtt_o[:, tg * 512:(tg + 1) * 512], evf[:]))
        p.emit(nc, sem, blk)
    return nc


def build_c2a():
    nc = bass.Bass("TRN2", target_bir_lowering=False)
    halo = nc.dram_tensor("halo", [CONV, TC + 3], BF16, kind="ExternalInput").ap()
    cw_in = nc.dram_tensor("cw_in", [128, 48 * 4], F32, kind="ExternalInput").ap()
    cb_in = nc.dram_tensor("cb_in", [128, 48], F32, kind="ExternalInput").ap()
    dtt_s = nc.dram_tensor("dtt_s", [64, TC], F32, kind="ExternalInput").ap()
    dtb_in = nc.dram_tensor("dtb_in", [64, 1], F32, kind="ExternalInput").ap()
    alog_in = nc.dram_tensor("alog_in", [64, 1], F32, kind="ExternalInput").ap()
    identb = nc.dram_tensor("identb", [128, 128], BF16, kind="ExternalInput").ap()
    identf = nc.dram_tensor("identf", [128, 128], F32, kind="ExternalInput").ap()

    xbct_o = nc.dram_tensor("xbct_o", [CONV, TC], BF16, kind="ExternalOutput").ap()
    snat_o = nc.dram_tensor("snat_o", [TC, 64], F32, kind="ExternalOutput").ap()
    dtpn_o = nc.dram_tensor("dtpn_o", [TC, 64], F32, kind="ExternalOutput").ap()
    sloc_o = nc.dram_tensor("sloc_o", [8 * 128, DM], BF16, kind="ExternalOutput").ap()
    mprime_o = nc.dram_tensor("mprime_o", [128, DM], BF16, kind="ExternalOutput").ap()
    cumdec_o = nc.dram_tensor("cumdec_o", [64, 9], F32, kind="ExternalOutput").ap()

    p = Prog()
    with ExitStack() as es:
        al = Alloc(nc, es)
        xbc_ = al.sb("xbc", [128, 48 * TC], BF16)
        hal = al.sb("hal", [128, TC + 3], BF16)
        cwb = al.sb("cwb", [128, 48 * 4], F32)
        cbb = al.sb("cbb", [128, 48], F32)
        cbuf = al.sb("cbuf", [128, TC], F32)
        dtT = al.sb("dtT", [64, TC], F32)
        dtb = al.sb("dtb", [64, 1], F32)
        acol = al.sb("acol", [64, 1], F32)
        zb = al.sb("zb", [64, 128], F32)
        rl = al.sb("rl", [64, 128], F32)
        ab = al.sb("ab", [64, 128], F32)
        eb = al.sb("eb", [64, 128], F32)
        lb = al.sb("lb", [64, 128], F32)
        dtpT = al.sb("dtpT", [64, 128], F32)
        am = al.sb("am", [64, 128], F32)
        sT = al.sb("sT", [64, 128], F32)
        z64 = al.sb("z64", [64, 128], F32)
        wtm = al.sb("wtm", [64, 128], F32)
        wT = al.sb("wT", [64, 128], F32)
        snat = al.sb("snat", [128, 64], F32)
        dtpn = al.sb("dtpn", [128, 64], F32)
        wnat = al.sb("wnat", [128, 64], F32)
        dcol = al.sb("dcol", [64, 1], F32)
        cum = al.sb("cum", [64, 9], F32)
        drow = al.sb("drow", [1, 64], F32)
        Dbc = al.sb("Dbc", [128, 64], F32)
        Sst = al.sb("Sst", [128, DM], F32)
        Sb = al.sb("Sb", [128, DM], BF16)
        bnat = al.sb("bnat", [128, 128], BF16)
        xnat = al.sb("xnat", [128, 64], BF16)
        dtxb = al.sb("dtxb", [128, 64], BF16)
        dtxw = al.sb("dtxw", [128, 64], BF16)
        onesf = al.sb("onesf", [1, 128], F32)
        idb = al.sb("idb", [128, 128], BF16)
        idf = al.sb("idf", [128, 128], F32)
        psC = al.ps("psC", [128, 64], F32)
        psRow = al.ps("psRow", [1, 128], F32)
        psBc = al.ps("psBc", [128, 64], F32)
        psTf = al.ps("psTf", [128, 64], F32)
        psTb = al.ps("psTb", [128, 128], BF16)
        sem = al.sem("sem")
        blk = al.block()
        xbc = _r(xbc_[:, :], "p (ch t) -> p ch t", ch=48)
        p.dma(lambda e: e.dma_start(idb[:], identb[:]))
        p.dma(lambda e: e.dma_start(idf[:], identf[:]))
        p.vec(lambda e: e.memset(onesf[:], 1.0))
        p.vec(lambda e: e.memset(z64[:], 0.0))
        p.dma(lambda e: e.dma_start(cwb[:], cw_in[:]))
        p.dma(lambda e: e.dma_start(cbb[:], cb_in[:]))
        p.dma(lambda e: e.dma_start(dtT[:], dtt_s[:]))
        p.dma(lambda e: e.dma_start(dtb[:], dtb_in[:]))
        p.dma(lambda e: e.dma_start(acol[:], alog_in[:]))
        p.act(lambda e: e.activation(acol[:], acol[:], AF.Exp))
        p.vec(lambda e: e.tensor_scalar(acol[:], acol[:], -1.0, None, op0=ALU.mult))
        for ch in range(48):
            p.dma(lambda e, ch=ch: e.dma_start(hal[:], halo[ch * 128:(ch + 1) * 128, :]))
            p.vec(lambda e, ch=ch: e.tensor_scalar(
                cbuf[:], hal[:, 0:TC], cwb[:, ch * 4:ch * 4 + 1], cbb[:, ch:ch + 1],
                op0=ALU.mult, op1=ALU.add))
            for k in range(1, 4):
                p.vec(lambda e, ch=ch, k=k: e.scalar_tensor_tensor(
                    cbuf[:], hal[:, k:k + TC], cwb[:, ch * 4 + k:ch * 4 + k + 1], cbuf[:],
                    op0=ALU.mult, op1=ALU.add))
            p.act(lambda e, ch=ch: e.activation(xbc[:, ch, :], cbuf[:], AF.Silu))
            p.dma(lambda e, ch=ch: e.dma_start(xbct_o[ch * 128:(ch + 1) * 128, :], xbc[:, ch, :]))
        p.vec(lambda e: e.memset(Sst[:], 0.0))
        p.vec(lambda e: e.memset(cum[:, 0:1], 1.0))
        for k in range(8):
            sl = slice(k * 128, (k + 1) * 128)
            p.vec(lambda e, sl=sl: e.tensor_scalar(zb[:], dtT[:, sl], dtb[:], None, op0=ALU.add))
            p.vec(lambda e: e.tensor_scalar(rl[:], zb[:], 0.0, None, op0=ALU.max))
            p.act(lambda e: e.activation(ab[:], zb[:], AF.Abs))
            p.act(lambda e: e.activation(eb[:], ab[:], AF.Exp, scale=-1.0))
            p.vec(lambda e: e.tensor_scalar(eb[:], eb[:], 1.0, None, op0=ALU.add))
            p.act(lambda e: e.activation(lb[:], eb[:], AF.Ln))
            p.vec(lambda e: e.tensor_tensor(dtpT[:], rl[:], lb[:], op=ALU.add))
            p.vec(lambda e: e.tensor_scalar(am[:], dtpT[:], acol[:], None, op0=ALU.mult))
            p.vec(lambda e: e.tensor_tensor_scan(sT[:], am[:], z64[:], 0.0, op0=ALU.add, op1=ALU.add))
            p.mm(lambda e: e.transpose(psTf[:], sT[:], idf[0:64, 0:64]))
            p.vec(lambda e: e.tensor_copy(snat[:], psTf[:]))
            p.dma(lambda e, sl=sl: e.dma_start(snat_o[sl, :], snat[:]))
            p.mm(lambda e: e.transpose(psTf[:], dtpT[:], idf[0:64, 0:64]))
            p.vec(lambda e: e.tensor_copy(dtpn[:], psTf[:]))
            p.dma(lambda e, sl=sl: e.dma_start(dtpn_o[sl, :], dtpn[:]))
            p.act(lambda e: e.activation(dcol[:], sT[:, 127:128], AF.Exp))
            p.vec(lambda e, k=k: e.tensor_tensor(cum[:, k + 1:k + 2], cum[:, k:k + 1], dcol[:], op=ALU.mult))
            p.vec(lambda e: e.tensor_scalar(wtm[:], sT[:], sT[:, 127:128], None, op0=ALU.subtract))
            p.act(lambda e: e.activation(wT[:], wtm[:], AF.Exp, scale=-1.0))
            p.mm(lambda e: e.transpose(psTf[:], wT[:], idf[0:64, 0:64]))
            p.vec(lambda e: e.tensor_copy(wnat[:], psTf[:]))
            p.mm(lambda e: e.transpose(psRow[0:1, 0:64], dcol[:], idf[0:64, 0:64]))
            p.vec(lambda e: e.tensor_copy(drow[:], psRow[0:1, 0:64]))
            p.mm(lambda e: e.matmul(psBc[:], onesf[:], drow[:], start=True, stop=True))
            p.vec(lambda e: e.tensor_copy(Dbc[:], psBc[:]))
            p.vec(lambda e: e.tensor_copy(Sb[:], Sst[:]))
            p.dma(lambda e, k=k: e.dma_start(sloc_o[k * 128:(k + 1) * 128, :], Sb[:]))
            for g in range(8):
                p.mm(lambda e, g=g, sl=sl: e.transpose(psTb[:], xbc[:, 32 + g, sl], idb[:]))
                p.vec(lambda e: e.tensor_copy(bnat[:], psTb[:]))
                for hh in range(8):
                    h = g * 8 + hh
                    pb = (h % 2) * 64
                    p.mm(lambda e, h=h, pb=pb, sl=sl: e.transpose(
                        psTb[0:128, 0:64], xbc_[pb:pb + 64, :].rearrange(
                            "p (ch t) -> p ch t", ch=48)[:, h // 2, sl], idb[pb:pb + 64, pb:pb + 64]))
                    p.vec(lambda e: e.tensor_copy(xnat[:], psTb[0:128, 0:64]))
                    p.vec(lambda e, h=h: e.tensor_scalar(dtxb[:], xnat[:], dtpn[:, h:h + 1], None, op0=ALU.mult))
                    p.vec(lambda e, h=h: e.tensor_scalar(dtxw[:], dtxb[:], wnat[:, h:h + 1], None, op0=ALU.mult))
                    p.mm(lambda e: e.matmul(psC[:], bnat[:], dtxw[:], start=True, stop=True))
                    p.vec(lambda e, h=h: e.scalar_tensor_tensor(
                        Sst[:, h * 64:(h + 1) * 64], Sst[:, h * 64:(h + 1) * 64],
                        Dbc[:, h:h + 1], psC[:], op0=ALU.mult, op1=ALU.add))
        p.vec(lambda e: e.tensor_copy(Sb[:], Sst[:]))
        p.dma(lambda e: e.dma_start(mprime_o[:], Sb[:]))
        p.dma(lambda e: e.dma_start(cumdec_o[:], cum[:]))
        p.emit(nc, sem, blk)
    return nc


def build_c2b():
    nc = bass.Bass("TRN2", target_bir_lowering=False)
    xbct_s = nc.dram_tensor("xbct_s", [CONV, TC], BF16, kind="ExternalInput").ap()
    gatet_s = nc.dram_tensor("gatet_s", [DM, TC], BF16, kind="ExternalInput").ap()
    snat_s = nc.dram_tensor("snat_s", [TC, 64], F32, kind="ExternalInput").ap()
    dtpn_s = nc.dram_tensor("dtpn_s", [TC, 64], F32, kind="ExternalInput").ap()
    sloc_s = nc.dram_tensor("sloc_s", [8 * 128, DM], BF16, kind="ExternalInput").ap()
    prevM = nc.dram_tensor("prevM", [7 * 128, DM], BF16, kind="ExternalInput").ap()
    prevD = nc.dram_tensor("prevD", [1, 7 * 64], F32, kind="ExternalInput").ap()
    cumd = nc.dram_tensor("cumd", [1, 9 * 64], F32, kind="ExternalInput").ap()
    d_in = nc.dram_tensor("d_in", [1, 64], F32, kind="ExternalInput").ap()
    nw_in = nc.dram_tensor("nw_in", [1, DM], F32, kind="ExternalInput").ap()
    triu_in = nc.dram_tensor("triu_in", [128, 128], F32, kind="ExternalInput").ap()
    identb = nc.dram_tensor("identb", [128, 128], BF16, kind="ExternalInput").ap()
    identf = nc.dram_tensor("identf", [128, 128], F32, kind="ExternalInput").ap()

    ynt_o = nc.dram_tensor("ynt_o", [DM, TC], BF16, kind="ExternalOutput").ap()

    p = Prog()
    with ExitStack() as es:
        al = Alloc(nc, es)
        xch_ = al.sb("xch", [128, 48 * 128], BF16)
        gch_ = al.sb("gch", [128, 32 * 128], BF16)
        Sinit = al.sb("Sinit", [128, DM], F32)
        Sin = al.sb("Sin", [128, DM], BF16)
        slocb = al.sb("slocb", [128, DM], BF16)
        pmb = al.sb("pmb", [128, DM], BF16)
        rowbuf = al.sb("rowbuf", [1, 576], F32)
        drow = al.sb("drow", [1, 64], F32)
        Dbcj = al.sb("Dbcj", [128, 64], F32)
        DbcD = al.sb("DbcD", [128, 64], F32)
        cumbc = al.sb("cumbc", [128, 64], F32)
        nwbc = al.sb("nwbc", [128, DM], F32)
        nwrow = al.sb("nwrow", [1, DM], F32)
        triu = al.sb("triu", [128, 128], F32)
        snc = al.sb("snc", [128, 64], F32)
        negs = al.sb("negs", [128, 64], F32)
        dtpn = al.sb("dtpn", [128, 64], F32)
        gm_ = al.sb("gm", [128, 8 * 128], F32)
        srow = al.sb("srow", [1, 128], F32)
        dbuf = al.sb("dbuf", [128, 128], F32)
        ebuf = al.sb("ebuf", [128, 128], F32)
        mtb = al.sb("mtb", [128, 128], BF16)
        epb = al.sb("epb", [128, 128], BF16)
        chb = al.sb("chb", [128, 128], BF16)
        xnat = al.sb("xnat", [128, 64], BF16)
        dtxb = al.sb("dtxb", [128, 64], BF16)
        ytb = al.sb("ytb", [64, 128], BF16)
        ytile = al.sb("ytile", [128, DM], F32)
        sqscr = al.sb("sqscr", [128, DM], F32)
        gnat = al.sb("gnat", [128, DM], BF16)
        ybf = al.sb("ybf", [128, DM], BF16)
        ms8 = al.sb("ms8", [128, 8], F32)
        rstd8 = al.sb("rstd8", [128, 8], F32)
        ytt = al.sb("ytt", [128, 128], BF16)
        onesf = al.sb("onesf", [1, 128], F32)
        idb = al.sb("idb", [128, 128], BF16)
        idf = al.sb("idf", [128, 128], F32)
        psG = al.ps("psG", [128, 128], F32)
        psB = al.ps("psB", [128, 128], F32)
        psRow = al.ps("psRow", [1, 128], F32)
        psY = al.ps("psY", [64, 128], F32)
        psTb = al.ps("psTb", [128, 128], BF16)
        psN = al.ps("psN", [128, 512], F32)
        sem = al.sem("sem")
        blk = al.block()
        xch = _r(xch_[:, :], "p (ch t) -> p ch t", ch=48)
        gch = _r(gch_[:, :], "p (cc t) -> p cc t", cc=32)
        gm = gm_
        p.dma(lambda e: e.dma_start(idb[:], identb[:]))
        p.dma(lambda e: e.dma_start(idf[:], identf[:]))
        p.dma(lambda e: e.dma_start(triu[:], triu_in[:]))
        p.vec(lambda e: e.memset(onesf[:], 1.0))
        p.dma(lambda e: e.dma_start(nwrow[:], nw_in[:]))
        for g in range(8):
            p.mm(lambda e, g=g: e.matmul(psN[:], onesf[:], nwrow[0:1, g * 512:(g + 1) * 512], start=True, stop=True))
            p.vec(lambda e, g=g: e.tensor_copy(nwbc[:, g * 512:(g + 1) * 512], psN[:]))
        p.dma(lambda e: e.dma_start(drow[:], d_in[:]))
        p.mm(lambda e: e.matmul(psB[:, 0:64], onesf[:], drow[:], start=True, stop=True))
        p.vec(lambda e: e.tensor_copy(DbcD[:], psB[:, 0:64]))
        p.vec(lambda e: e.memset(Sinit[:], 0.0))
        p.dma(lambda e: e.dma_start(rowbuf[0:1, 0:448], prevD[:]))
        for j in range(7):
            p.dma(lambda e, j=j: e.dma_start(pmb[:], prevM[j * 128:(j + 1) * 128, :]))
            p.mm(lambda e, j=j: e.matmul(psB[:, 0:64], onesf[:], rowbuf[0:1, j * 64:(j + 1) * 64], start=True, stop=True))
            p.vec(lambda e: e.tensor_copy(Dbcj[:], psB[:, 0:64]))
            for h in range(64):
                p.vec(lambda e, h=h: e.scalar_tensor_tensor(
                    Sinit[:, h * 64:(h + 1) * 64], Sinit[:, h * 64:(h + 1) * 64],
                    Dbcj[:, h:h + 1], pmb[:, h * 64:(h + 1) * 64], op0=ALU.mult, op1=ALU.add))
        p.dma(lambda e: e.dma_start(rowbuf[:], cumd[:]))

        for k in range(8):
            sl = slice(k * 128, (k + 1) * 128)
            xbv = _r(xbct_s, "(ch p) t -> p ch t", p=128)
            gtv = _r(gatet_s, "(cc p) t -> p cc t", p=128)
            for q in range(3):
                p.dma(lambda e, sl=sl, q=q, xbv=xbv: e.dma_start(
                    xch[:, q * 16:(q + 1) * 16, :], xbv[:, q * 16:(q + 1) * 16, sl]))
            for q in range(2):
                p.dma(lambda e, sl=sl, q=q, gtv=gtv: e.dma_start(
                    gch[:, q * 16:(q + 1) * 16, :], gtv[:, q * 16:(q + 1) * 16, sl]))
            p.dma(lambda e, k=k: e.dma_start(slocb[:], sloc_s[k * 128:(k + 1) * 128, :]))
            p.dma(lambda e, sl=sl: e.dma_start(snc[:], snat_s[sl, :]))
            p.dma(lambda e, sl=sl: e.dma_start(dtpn[:], dtpn_s[sl, :]))
            p.mm(lambda e, k=k: e.matmul(psB[:, 0:64], onesf[:], rowbuf[0:1, k * 64:(k + 1) * 64], start=True, stop=True))
            p.vec(lambda e: e.tensor_copy(cumbc[:], psB[:, 0:64]))
            for h in range(64):
                p.vec(lambda e, h=h: e.scalar_tensor_tensor(
                    Sin[:, h * 64:(h + 1) * 64], Sinit[:, h * 64:(h + 1) * 64],
                    cumbc[:, h:h + 1], slocb[:, h * 64:(h + 1) * 64], op0=ALU.mult, op1=ALU.add))
            p.vec(lambda e: e.tensor_scalar(negs[:], snc[:], -1.0, None, op0=ALU.mult))
            for g in range(8):
                p.mm(lambda e, g=g: e.matmul(psG[:], xch[:, 32 + g, :], xch[:, 40 + g, :], start=True, stop=True))
                p.vec(lambda e, g=g: e.tensor_tensor(gm[:, g * 128:(g + 1) * 128], psG[:], triu[:], op=ALU.mult))
            for h in range(64):
                g = h // 8
                pb = (h % 2) * 64
                p.mm(lambda e, h=h: e.transpose(psRow[:], snc[:, h:h + 1], idf[:]))
                p.vec(lambda e: e.tensor_copy(srow[:], psRow[:]))
                p.mm(lambda e: e.matmul(psB[:], onesf[:], srow[:], start=True, stop=True))
                p.vec(lambda e, h=h: e.scalar_tensor_tensor(
                    dbuf[:], psB[:], negs[:, h:h + 1], triu[:], op0=ALU.add, op1=ALU.mult))
                p.act(lambda e: e.activation(ebuf[:], dbuf[:], AF.Exp))
                p.vec(lambda e, g=g: e.tensor_tensor(mtb[:], gm[:, g * 128:(g + 1) * 128], ebuf[:], op=ALU.mult))
                p.act(lambda e: e.activation(epb[:], psB[:], AF.Exp))
                p.vec(lambda e, g=g: e.tensor_tensor(chb[:], xch[:, 40 + g, :], epb[:], op=ALU.mult))
                p.mm(lambda e, h=h, pb=pb: e.transpose(
                    psTb[0:128, 0:64], xch_[pb:pb + 64, :].rearrange(
                        "p (ch t) -> p ch t", ch=48)[:, h // 2, :], idb[pb:pb + 64, pb:pb + 64]))
                p.vec(lambda e: e.tensor_copy(xnat[:], psTb[0:128, 0:64]))
                p.vec(lambda e, h=h: e.tensor_scalar(dtxb[:], xnat[:], dtpn[:, h:h + 1], None, op0=ALU.mult))
                p.mm(lambda e: e.matmul(psY[:], dtxb[:], mtb[:], start=True, stop=False))
                p.mm(lambda e, h=h: e.matmul(psY[:], Sin[:, h * 64:(h + 1) * 64], chb[:], start=False, stop=True))
                p.vec(lambda e: e.tensor_copy(ytb[:], psY[:]))
                p.mm(lambda e: e.transpose(psTb[0:128, 0:64], ytb[:], idb[0:64, 0:64]))
                p.vec(lambda e, h=h: e.scalar_tensor_tensor(
                    ytile[:, h * 64:(h + 1) * 64], xnat[:], DbcD[:, h:h + 1],
                    psTb[0:128, 0:64], op0=ALU.mult, op1=ALU.add))
            p.act(lambda e: e.activation(sqscr[:], ytile[:], AF.Square))
            p.vec(lambda e: e.tensor_reduce(
                _r(ms8[:, :], "p g -> p g").unsqueeze(2).squeeze(2),
                _r(sqscr[:, :], "p (g c) -> p g c", g=8), axis=AX.X, op=ALU.add))
            p.vec(lambda e: e.tensor_scalar(ms8[:], ms8[:], 1.0 / 512, 1e-5, op0=ALU.mult, op1=ALU.add))
            p.act(lambda e: e.activation(rstd8[:], ms8[:], AF.Ln))
            p.act(lambda e: e.activation(ms8[:], rstd8[:], AF.Exp, scale=-0.5))
            for g in range(8):
                p.vec(lambda e, g=g: e.scalar_tensor_tensor(
                    ytile[:, g * 512:(g + 1) * 512], ytile[:, g * 512:(g + 1) * 512],
                    ms8[:, g:g + 1], nwbc[:, g * 512:(g + 1) * 512], op0=ALU.mult, op1=ALU.mult))
            for cc in range(32):
                p.mm(lambda e, cc=cc: e.transpose(psTb[:], gch[:, cc, :], idb[:]))
                p.vec(lambda e, cc=cc: e.tensor_copy(gnat[:, cc * 128:(cc + 1) * 128], psTb[:]))
            p.act(lambda e: e.activation(sqscr[:], gnat[:], AF.Silu))
            p.vec(lambda e: e.tensor_tensor(ytile[:], ytile[:], sqscr[:], op=ALU.mult))
            p.vec(lambda e: e.tensor_copy(ybf[:], ytile[:]))
            for cc in range(32):
                p.mm(lambda e, cc=cc: e.transpose(psTb[:], ybf[:, cc * 128:(cc + 1) * 128], idb[:]))
                p.vec(lambda e: e.tensor_copy(ytt[:], psTb[:]))
                p.dma(lambda e, cc=cc, k=k: e.dma_start(
                    ynt_o[cc * 128:(cc + 1) * 128, k * 128:(k + 1) * 128], ytt[:]))
        p.emit(nc, sem, blk)
    return nc


def build_c3():
    nc = bass.Bass("TRN2", target_bir_lowering=False)
    xn1t_s = nc.dram_tensor("xn1t_s", [DM, TC], BF16, kind="ExternalInput").ap()
    ynt_s = nc.dram_tensor("ynt_s", [DM, TC], BF16, kind="ExternalInput").ap()
    opT = nc.dram_tensor("opT", [DM, DM], BF16, kind="ExternalInput").ap()
    f1T = nc.dram_tensor("f1T", [DM, DM], BF16, kind="ExternalInput").ap()
    f1b = nc.dram_tensor("f1b", [1, DM], F32, kind="ExternalInput").ap()
    x1_s = nc.dram_tensor("x1_s", [TC, DM], F32, kind="ExternalInput").ap()

    out_o = nc.dram_tensor("out_o", [TC, DM], F32, kind="ExternalOutput").ap()

    p = Prog()
    with ExitStack() as es:
        al = Alloc(nc, es)
        xnT_ = al.sb("xnT", [128, 32 * TC], BF16)
        opw_ = al.sb("opw", [128, 32 * 512], BF16)
        f1w_ = al.sb("f1w", [128, 32 * 512], BF16)
        ynT_ = al.sb("ynT", [128, 32 * 128], BF16)
        x1scr = al.sb("x1scr", [128, 512], F32)
        osb = al.sb("osb", [128, 512], F32)
        rowf = al.sb("rowf", [1, DM], F32)
        f1bb = al.sb("f1bb", [1, DM], BF16)
        onesb = al.sb("onesb", [1, 512], BF16)
        psA = al.ps("psA", [128, 512], F32)
        psB2 = al.ps("psB2", [128, 512], F32)
        sem = al.sem("sem")
        blk = al.block()
        xnT = _r(xnT_[:, :], "p (cc t) -> p cc t", cc=32)
        opw = _r(opw_[:, :], "p (cc m) -> p cc m", cc=32)
        f1w = _r(f1w_[:, :], "p (cc m) -> p cc m", cc=32)
        ynT = _r(ynT_[:, :], "p (cc t) -> p cc t", cc=32)
        xnv = _r(xn1t_s, "(cc p) t -> p cc t", p=128)
        for q in range(4):
            p.dma(lambda e, q=q: e.dma_start(xnT[:, q * 8:(q + 1) * 8, :], xnv[:, q * 8:(q + 1) * 8, :]))
        p.dma(lambda e: e.dma_start(rowf[:], f1b[:]))
        p.vec(lambda e: e.tensor_copy(f1bb[:], rowf[:]))
        p.vec(lambda e: e.memset(onesb[:], 1.0))
        opv = _r(opT, "(cc p) m -> p cc m", p=128)
        f1v = _r(f1T, "(cc p) m -> p cc m", p=128)
        ynv = _r(ynt_s, "(cc p) t -> p cc t", p=128)
        for dmg in range(8):
            for q in range(2):
                p.dma(lambda e, dmg=dmg, q=q: e.dma_start(
                    opw[:, q * 16:(q + 1) * 16, :], opv[:, q * 16:(q + 1) * 16, dmg * 512:(dmg + 1) * 512]))
                p.dma(lambda e, dmg=dmg, q=q: e.dma_start(
                    f1w[:, q * 16:(q + 1) * 16, :], f1v[:, q * 16:(q + 1) * 16, dmg * 512:(dmg + 1) * 512]))
            for ti in range(NT):
                for q in range(2):
                    p.dma(lambda e, ti=ti, q=q: e.dma_start(
                        ynT[:, q * 16:(q + 1) * 16, :], ynv[:, q * 16:(q + 1) * 16, ti * 128:(ti + 1) * 128]))
                p.dma(lambda e, ti=ti, dmg=dmg: e.dma_start(
                    x1scr[:], x1_s[ti * 128:(ti + 1) * 128, dmg * 512:(dmg + 1) * 512]))
                for cc in range(32):
                    p.mm(lambda e, cc=cc: e.matmul(psA[:], ynT[:, cc, :], opw[:, cc, :],
                                                   start=(cc == 0), stop=(cc == 31)))
                p.mm(lambda e, dmg=dmg: e.matmul(psB2[:], onesb[0:1, 0:128],
                                                 f1bb[0:1, dmg * 512:(dmg + 1) * 512],
                                                 start=True, stop=False))
                for cc in range(32):
                    p.mm(lambda e, cc=cc, ti=ti: e.matmul(
                        psB2[:], xnT[:, cc, ti * 128:(ti + 1) * 128], f1w[:, cc, :],
                        start=False, stop=(cc == 31)))
                p.vec(lambda e: e.tensor_tensor(osb[:], psA[:], x1scr[:], op=ALU.add))
                p.vec(lambda e: e.tensor_tensor(osb[:], osb[:], psB2[:], op=ALU.add))
                p.dma(lambda e, ti=ti, dmg=dmg: e.dma_start(
                    out_o[ti * 128:(ti + 1) * 128, dmg * 512:(dmg + 1) * 512], osb[:]))
        p.emit(nc, sem, blk)
    return nc


def kernel(L0_attn_x, L0_attn_Wqkv, L0_attn_bqkv, L0_attn_Wo, L0_attn_bo,
           L0_ffn_W_res, L0_ffn_W_pre, L0_ffn_W_post, L0_ffn_H_res_base,
           L0_ffn_H_pre_base, L0_ffn_H_post_base, L0_ffn_gate, L0_ffn_W_branch,
           L1_attn_in_proj_w, L1_attn_conv_weight, L1_attn_conv_bias,
           L1_attn_dt_bias, L1_attn_A_log, L1_attn_D, L1_attn_norm_weight,
           L1_attn_out_proj_w, L1_ffn_qweight, L1_ffn_scales, L1_ffn_scaled_zeros,
           L1_ffn_bias):
    cores = list(range(NC))
    identf = np.eye(128, dtype=np.float32)
    identb = np.eye(128, dtype=BF)
    triu = np.triu(np.ones((128, 128), dtype=np.float32))

    def c(a):
        return np.ascontiguousarray(a)

    ip_pad = np.concatenate([L1_attn_in_proj_w,
                             np.zeros((11264 - PROJ, DM), np.float32)], axis=0)
    maps = []
    for i in cores:
        maps.append({
            "wqkv_s": c(L0_attn_Wqkv[i * 512:(i + 1) * 512]),
            "wo_s": c(L0_attn_Wo[i * 128:(i + 1) * 128]),
            "wbr_s": c(L0_ffn_W_branch[i * 128:(i + 1) * 128]),
            "ip_s": c(ip_pad[i * 1408:(i + 1) * 1408]),
            "op_s": c(L1_attn_out_proj_w[i * 512:(i + 1) * 512]),
            "qw_s": c(L1_ffn_qweight.reshape(DM, DM)[i * 512:(i + 1) * 512]),
            "sc_s": c(L1_ffn_scales.reshape(DM, 32)[i * 512:(i + 1) * 512]),
            "zp_s": c(L1_ffn_scaled_zeros.reshape(DM, 32)[i * 512:(i + 1) * 512]),
            "identb": identb,
        })
    r0 = run_bass_kernel_spmd(build_p0(), maps, core_ids=cores).results
    wqkv_b = np.concatenate([r["wqkv_o"] for r in r0], axis=0)
    wo_b = np.concatenate([r["wo_o"] for r in r0], axis=0)
    wbr_b = np.concatenate([r["wbr_o"] for r in r0], axis=0)
    ipT = np.concatenate([r["ipT_o"] for r in r0], axis=1)[:, 0:PROJ]
    opT = np.concatenate([r["opT_o"] for r in r0], axis=1)
    f1T = np.concatenate([r["f1T_o"] for r in r0], axis=1)

    x = L0_attn_x.reshape(S, DM)
    wcat = np.concatenate([L0_ffn_W_res, L0_ffn_W_pre, L0_ffn_W_post], axis=0)
    bases = np.concatenate([L0_ffn_H_res_base.reshape(-1), L0_ffn_H_pre_base,
                            L0_ffn_H_post_base])[None, :]
    maps = []
    for i in cores:
        maps.append({
            "x_s": c(x[i * TC:(i + 1) * TC]),
            "wqkv_b": wqkv_b, "bqkv": c(L0_attn_bqkv[None, :]),
            "wcat": c(wcat), "bases": c(bases),
            "gatein": c(L0_ffn_gate[None, :]),
            "eye16": c(np.eye(4, dtype=np.float32).reshape(1, 16)),
            "wbr_b": wbr_b, "identb": identb, "identf": identf,
        })
    ra = run_bass_kernel_spmd(build_a(), maps, core_ids=cores).results
    kt_f = np.concatenate([r["qkt_o"][1024:2048] for r in ra], axis=1)
    vn_f = np.concatenate([r["vn_o"] for r in ra], axis=0)

    maps = []
    for i in cores:
        maps.append({
            "qt_s": c(ra[i]["qkt_o"][0:1024]),
            "kt_f": kt_f, "vn_f": vn_f,
            "part_s": ra[i]["part_o"],
            "wo_b": wo_b, "bo": c(L0_attn_bo[None, :]), "identb": identb,
        })
    rb = run_bass_kernel_spmd(build_b(), maps, core_ids=cores).results

    maps = []
    for i in cores:
        maps.append({"xn1t_s": rb[i]["xn1t_o"], "ipT": ipT})
    rc1 = run_bass_kernel_spmd(build_c1(), maps, core_ids=cores).results

    hbct = np.concatenate([r["hbct_o"] for r in rc1], axis=1)
    maps = []
    for i in cores:
        if i == 0:
            halo = np.concatenate([np.zeros((CONV, 3), BF), hbct[:, 0:TC]], axis=1)
        else:
            halo = hbct[:, i * TC - 3:(i + 1) * TC]
        maps.append({
            "halo": c(halo),
            "cw_in": c(L1_attn_conv_weight.reshape(48, 128, 4).transpose(1, 0, 2).reshape(128, 192)),
            "cb_in": c(L1_attn_conv_bias.reshape(48, 128).T),
            "dtt_s": rc1[i]["dtt_o"],
            "dtb_in": c(L1_attn_dt_bias[:, None]),
            "alog_in": c(L1_attn_A_log[:, None]),
            "identb": identb, "identf": identf,
        })
    rc2a = run_bass_kernel_spmd(build_c2a(), maps, core_ids=cores).results

    mprimes = [r["mprime_o"] for r in rc2a]
    dtots = [r["cumdec_o"][:, 8] for r in rc2a]
    maps = []
    for i in cores:
        pm = np.zeros((7 * 128, DM), BF)
        pd = np.ones((7, 64), np.float32)
        nprev = i
        for j in range(nprev):
            s0 = 7 - nprev + j
            pm[s0 * 128:(s0 + 1) * 128] = mprimes[j]
            pd[s0] = dtots[j]
        cumd = rc2a[i]["cumdec_o"].T.reshape(1, 9 * 64)
        maps.append({
            "xbct_s": rc2a[i]["xbct_o"],
            "gatet_s": rc1[i]["gatet_o"],
            "snat_s": rc2a[i]["snat_o"],
            "dtpn_s": rc2a[i]["dtpn_o"],
            "sloc_s": rc2a[i]["sloc_o"],
            "prevM": pm, "prevD": c(pd.reshape(1, 448)),
            "cumd": c(cumd),
            "d_in": c(L1_attn_D[None, :]),
            "nw_in": c(L1_attn_norm_weight[None, :]),
            "triu_in": triu, "identb": identb, "identf": identf,
        })
    rc2b = run_bass_kernel_spmd(build_c2b(), maps, core_ids=cores).results

    maps = []
    for i in cores:
        maps.append({
            "xn1t_s": rb[i]["xn1t_o"], "ynt_s": rc2b[i]["ynt_o"],
            "opT": opT, "f1T": f1T,
            "f1b": c(L1_ffn_bias[None, :]),
            "x1_s": rb[i]["x1_o"],
        })
    rc3 = run_bass_kernel_spmd(build_c3(), maps, core_ids=cores).results
    out = np.concatenate([r["out_o"] for r in rc3], axis=0)
    return out.reshape(1, S, DM).astype(np.float32)
